# version 18
# baseline (speedup 1.0000x reference)
import numpy as np
import ml_dtypes
import concourse.bass as bass
import concourse.mybir as mybir
from concourse.bass_utils import run_bass_kernel_spmd
from contextlib import ExitStack

F32 = mybir.dt.float32
BF16 = mybir.dt.bfloat16
I32 = mybir.dt.int32
AF = mybir.ActivationFunctionType
ALU = mybir.AluOpType
AX = mybir.AxisListType

MAGIC = 1414213
CH = 512
NCHUNK = 32
G = 8
NKC = 8
HB = 32
NST = 16
DM = 1024
DHID = 4096
SEQ = 16384
QTOK = 4096
BW = 1032
EPS_ATTN = 1e-6
EPS_OUT = 1e-5
FT = 256
ENGINES = ("sync", "scalar", "vector", "tensor", "gpsimd")
DEBUG = False
DEBUG2 = False
TESTDELAY = False
DEBUG3 = False


class Sched:
    def __init__(self):
        self.ops = {e: [] for e in ENGINES}
        self.count = {}

    def emit(self, e, fn):
        self.ops[e].append(fn)

    def inc(self, s, amount):
        v = self.count.get(s, 0) + amount
        self.count[s] = v
        return v

    def cur(self, s):
        return self.count.get(s, 0)


def _stage_vectors(weights):
    S_, dim = weights.shape[0], weights.shape[1] * 2
    wd = np.zeros((S_, dim), np.float32)
    wo = np.zeros((S_, dim), np.float32)
    for st in range(S_):
        s = 2 ** st
        nb = dim // (2 * s)
        w = weights[st].reshape(nb, s, 2, 2)
        i = np.arange(dim)
        blk = i // (2 * s)
        sub = (i % (2 * s)) % s
        hi = (i % (2 * s)) // s
        wd[st] = np.where(hi == 0, w[blk, sub, 0, 0], w[blk, sub, 1, 1])
        wo[st] = np.where(hi == 0, w[blk, sub, 0, 1], w[blk, sub, 1, 0])
    return wd, wo


def _perm(v):
    S_, dim = v.shape
    out = np.empty_like(v)
    for s in range(S_):
        out[s] = v[s][np.arange(dim) ^ (2 ** s)]
    return out


def _pack_sv(v):
    return np.ascontiguousarray(
        v.reshape(12, 32, 128).transpose(2, 0, 1).reshape(128, 12 * 32))


def build_program():
    nc = bass.Bass("TRN2", target_bir_lowering=False)

    def din(name, shape, dt=F32):
        return nc.dram_tensor(name, shape, dt, kind="ExternalInput")

    x_full = din("x_full", [SEQ, DM])
    x_own = din("x_own", [QTOK, DM])
    Wi = din("Wi", [DM, 2048])
    convw = din("convw", [128, G * 4])
    convb = din("convb", [128, G])
    Wxp = din("Wxp", [1024, 96])
    Wdt = din("Wdt", [64, 1024])
    bdt = din("bdt", [128, G])
    Alog = din("Alog", [128, G * NST])
    Dp = din("Dp", [128, G])
    Wout = din("Wout", [2048, DM])
    gamma = din("gamma", [1, DM])
    beta = din("beta", [1, DM])
    Wexp = din("Wexp", [DM, DHID])
    bexp = din("bexp", [1, DHID])
    upb = din("upb", [128, HB])
    Wcon = din("Wcon", [DHID, DM])
    bcon = din("bcon", [1, DM])
    downb = din("downb", [1, DM])
    wdU = din("wdU", [128, 12 * HB]); woU = din("woU", [128, 12 * HB]); woUp = din("woUp", [128, 12 * HB])
    wdD = din("wdD", [128, 12 * HB]); woD = din("woD", [128, 12 * HB]); woDp = din("woDp", [128, 12 * HB])
    identb_in = din("identb", [128, 128], BF16)
    identf_in = din("identf", [128, 128])
    perms_in = din("perms", [128, 7 * 128])
    ones1_in = din("ones1", [1, 128])
    selmat_in = din("selmat", [32, 32 * 128], BF16)
    role_in = din("role", [1, 8], I32)
    nonce_in = din("nonce", [1, 16], I32)

    mout = nc.dram_tensor("mout", [QTOK, DM], F32, kind="ExternalOutput")
    if DEBUG or DEBUG2 or DEBUG3:
        d_wexp = nc.dram_tensor("d_wexp", [128, NKC * DHID], BF16, kind="ExternalOutput")
        d_wcon = nc.dram_tensor("d_wcon", [128, HB * DM], BF16, kind="ExternalOutput")
        d_x1 = nc.dram_tensor("d_x1", [QTOK, DM], F32, kind="ExternalOutput")
        d_gated = nc.dram_tensor("d_gated", [2048, 8192], BF16, kind="ExternalOutput")
        d_xdbl = nc.dram_tensor("d_xdbl", [2 * NCHUNK * 96, CH], F32, kind="ExternalOutput")
        d_m = nc.dram_tensor("d_m", [QTOK, DM], F32, kind="ExternalOutput")
        d_gld = nc.dram_tensor("d_gld", [128, 16 * 128], BF16, kind="ExternalOutput")
        d_stat = nc.dram_tensor("d_stat", [32 * 128, 8], F32, kind="ExternalOutput")
        d_beff = nc.dram_tensor("d_beff", [128, HB], F32, kind="ExternalOutput")
        d_lnu2 = nc.dram_tensor("d_lnu2", [256, DM], F32, kind="ExternalOutput")
        d_gelt = nc.dram_tensor("d_gelt", [128, HB * FT], BF16, kind="ExternalOutput")
        d_x1fm = nc.dram_tensor("d_x1fm", [128, NKC * FT], BF16, kind="ExternalOutput")
        d_wexpS = nc.dram_tensor("d_wexpS", [128, NKC * DHID], BF16, kind="ExternalOutput")
        d_wexpS2 = nc.dram_tensor("d_wexpS2", [128, NKC * DHID], BF16, kind="ExternalOutput")
        d_wexp_pre = nc.dram_tensor("d_wexp_pre", [128, NKC * DHID], BF16, kind="ExternalOutput")
        d_hp = nc.dram_tensor("d_hp", [128, FT], F32, kind="ExternalOutput")
        d_yp = nc.dram_tensor("d_yp", [128, DM], F32, kind="ExternalOutput")
        d_x1t = nc.dram_tensor("d_x1t", [128, 2 * DM], F32, kind="ExternalOutput")
        d_lnu = nc.dram_tensor("d_lnu", [QTOK, DM], F32, kind="ExternalOutput")

    sh_xdbl = nc.dram_tensor("sh_xdbl", [2, NCHUNK, 96, CH], F32, addr_space="Shared")
    sh_gated = nc.dram_tensor("sh_gated", [2048, 8192], BF16, addr_space="Shared")
    sh_flags = nc.dram_tensor("sh_flags", [2, 64, 16], I32, addr_space="Shared")
    wexp_d = nc.dram_tensor("wexp_d", [128, NKC * DHID], BF16)
    wcon_d = nc.dram_tensor("wcon_d", [128, HB * DM], BF16)
    x1_d = nc.dram_tensor("x1_d", [QTOK, DM], F32)
    gbc_d = nc.dram_tensor("gbc_d", [128, DM], F32)
    bbc_d = nc.dram_tensor("bbc_d", [128, DM], F32)
    b2bc_d = nc.dram_tensor("b2bc_d", [128, DM], F32)

    S = Sched()

    with ExitStack() as ctx:
        def sb(name, shape, dt=F32, c=None):
            return (c or ctx).enter_context(nc.sbuf_tensor(name, shape, dt))

        def mkpsum(name, shape, dt=F32):
            return ctx.enter_context(nc.psum_tensor(name, shape, dt))

        Atile = sb("Atile", [128, G * NST])
        convwS = sb("convwS", [128, G * 4])
        convbS = sb("convbS", [128, G])
        bdtS = sb("bdtS", [128, G])
        DpS = sb("DpS", [128, G])
        identb = sb("identb_s", [128, 128], BF16)
        ones1 = sb("ones1_s", [1, 128])
        roleS = sb("roleS", [1, 8], I32)
        magicS = sb("magicS", [1, 16], I32)
        beff = sb("beff", [128, HB])
        epsA = sb("epsA", [128, 1])
        epsO = sb("epsO", [128, 1])
        carry = sb("carry", [128, G * NST])
        flg = sb("flg", [1, 16], I32)

        pA = mkpsum("pA", [128, 1024])
        pB = mkpsum("pB", [128, 1024])
        pC = mkpsum("pC", [128, 512])
        pD = mkpsum("pD", [128, 512])
        pE = mkpsum("pE", [128, 1024], BF16)
        pF = mkpsum("pF", [128, 1024], BF16)

        sems = {}
        for s_ in ("LD", "GD", "PL", "PE", "DV", "AC", "PO", "ST"):
            sems[s_] = ctx.enter_context(nc.semaphore(s_))

        regs = {}
        for rn, eng_ in (("g_h", nc.gpsimd), ("g_cnt", nc.gpsimd), ("g_flag", nc.gpsimd),
                         ("g_go", nc.gpsimd), ("g_ro", nc.gpsimd), ("g_nonce", nc.gpsimd),
                         ("s_h", nc.sync), ("v_ro", nc.vector), ("t_ro", nc.tensor)):
            regs[rn] = ctx.enter_context(eng_.register(rn))

        def E(e, fn):
            S.emit(e, fn)

        def wait(e, sname, val):
            if val > 0:
                E(e, lambda eng, s=sems[sname], v=val: eng.wait_ge(s, v))

        def dma(e, out_ap, in_ap, sname):
            v = S.inc(sname, 16)
            E(e, lambda eng, o=out_ap, i=in_ap, s=sems[sname]:
              eng.dma_start(out=o, in_=i).then_inc(s, 16))
            return v

        def op(e, fn, sname):
            v = S.inc(sname, 1)
            E(e, lambda eng, f=fn, s=sems[sname]: f(eng).then_inc(s, 1))
            return v

        def opn(e, fn):
            E(e, fn)

        def opd(e, fn, sname="DV"):
            v = op(e, fn, sname)
            wait(e, sname, v)
            return v

        _guards = {e: [] for e in ENGINES}

        def g_if(e, regname, val):
            def fn(eng, rn=regname, v=val):
                gd = eng.If_eq(regs[rn], v)
                gd.__enter__()
                _guards[e].append(gd)
            E(e, fn)

        def g_else(e):
            def fn(eng, ee=e):
                _guards[ee].pop().__exit__(None, None, None)
                gd = eng.Else()
                gd.__enter__()
                _guards[ee].append(gd)
            E(e, fn)

        def g_end(e):
            def fn(eng, ee=e):
                _guards[ee].pop().__exit__(None, None, None)
            E(e, fn)

        def balance(e, sname, amount):
            E(e, lambda eng, s=sems[sname], a=amount: eng.sem_inc(s, a))

        def cdma_h(e, out0, in0, out1, in1, sname):
            hreg = "g_h" if e == "gpsimd" else "s_h"
            v = S.inc(sname, 16)
            g_if(e, hreg, 0)
            E(e, lambda eng, o=out0, i=in0, s=sems[sname]:
              eng.dma_start(out=o, in_=i).then_inc(s, 16))
            g_else(e)
            E(e, lambda eng, o=out1, i=in1, s=sems[sname]:
              eng.dma_start(out=o, in_=i).then_inc(s, 16))
            g_end(e)
            return v

        bankfree = {}

        def pe_acquire(*banks):
            for bank in banks:
                ev = bankfree.get(bank)
                if ev:
                    wait("tensor", ev[0], ev[1])

        def bank_freed(bank, sname, val):
            bankfree[bank] = (sname, val)

        def poll(cell0, cell1):
            def fn(eng):
                eng.reg_mov(regs["g_go"], 1)
                wg = eng.While(regs["g_go"])
                wg.__enter__()
                with eng.If_eq(regs["g_h"], 0):
                    eng.dma_start(out=flg[:, :], in_=cell1).then_inc(sems["PL"], 16)
                with eng.Else():
                    eng.dma_start(out=flg[:, :], in_=cell0).then_inc(sems["PL"], 16)
                eng.reg_add(regs["g_cnt"], regs["g_cnt"], 16)
                eng.wait_ge(sems["PL"], regs["g_cnt"])
                eng.reg_load(regs["g_flag"], flg[:1, :1])
                eng.reg_sub(regs["g_go"], regs["g_flag"], regs["g_nonce"])
                wg.__exit__(None, None, None)
            E("gpsimd", fn)

        mctx = ExitStack()
        wiS = sb("wiS", [128, 16 * NKC * 128], BF16, c=mctx)
        wxpS = sb("wxpS", [128, G * 96], BF16, c=mctx)
        wdtS = sb("wdtS", [64, G * 128], BF16, c=mctx)

        pctx = ExitStack()
        identf = sb("identf_s", [128, 128], c=pctx)
        permsS = sb("permsS", [128, 7 * 128], c=pctx)
        grow = sb("grow", [1, DM], c=pctx); brow = sb("brow", [1, DM], c=pctx)
        b2row = sb("b2row", [1, DM], c=pctx); dbrow = sb("dbrow", [1, DM], c=pctx)
        wdUS = sb("wdUS", [128, 12 * HB], c=pctx); woUS = sb("woUS", [128, 12 * HB], c=pctx)
        woUPS = sb("woUPS", [128, 12 * HB], c=pctx)
        wdDS = sb("wdDS", [128, 12 * HB], c=pctx); woDS = sb("woDS", [128, 12 * HB], c=pctx)
        woDPS = sb("woDPS", [128, 12 * HB], c=pctx)
        cbuf = sb("cbuf", [128, HB * BW], BF16, c=pctx)
        smat = sb("smat", [128, 128], c=pctx)
        smat2 = sb("smat2", [128, 128], c=pctx)
        smatb = sb("smatb", [128, 128], BF16, c=pctx)
        smatb2 = sb("smatb2", [128, 128], BF16, c=pctx)
        wtmp = sb("wtmp", [128, DHID], BF16, c=pctx)

        for dst, src in ((roleS, role_in), (magicS, nonce_in), (identf, identf_in), (permsS, perms_in),
                         (ones1, ones1_in), (Atile, Alog), (convwS, convw),
                         (convbS, convb), (bdtS, bdt), (DpS, Dp), (identb, identb_in),
                         (grow, gamma), (brow, beta), (b2row, bcon), (dbrow, downb),
                         (wdUS, wdU), (woUS, woU), (woUPS, woUp),
                         (wdDS, wdD), (woDS, woD), (woDPS, woDp), (beff, upb)):
            dma("sync", dst[:, :], src.ap(), "LD")
        ld0 = S.cur("LD")
        for kc in range(NKC):
            dma("gpsimd",
                bass.AP(wiS, kc * 128,
                        [[16384, 128], [1024, 16], [1, 128]]),
                bass.AP(Wi, kc * 128 * 2048,
                        [[2048, 128], [128, 16], [1, 128]]), "GD")
        dma("gpsimd", wxpS[:, :],
            bass.AP(Wxp, 0, [[96, 128], [128 * 96, G], [1, 96]]), "GD")
        dma("gpsimd", wdtS[:, :], Wdt.ap(), "GD")
        gd_w = S.cur("GD")

        wait("gpsimd", "LD", ld0)
        opn("gpsimd", lambda eng: eng.reg_load(regs["g_h"], roleS[:1, 0:1]))
        opn("gpsimd", lambda eng: eng.reg_load(regs["g_nonce"], magicS[:1, 0:1]))
        opn("gpsimd", lambda eng: eng.reg_mov(regs["g_cnt"], 0))
        wait("sync", "LD", ld0)
        opn("sync", lambda eng: eng.reg_load(regs["s_h"], roleS[:1, 0:1]))
        wait("vector", "LD", ld0)
        wait("scalar", "LD", ld0)
        wait("tensor", "LD", ld0)

        aa = op("scalar", lambda eng: eng.activation(Atile[:, :], Atile[:, :], AF.Exp), "AC")
        wait("vector", "AC", aa)
        op("vector", lambda eng: eng.tensor_scalar_mul(Atile[:, :], Atile[:, :], -1.0), "DV")
        op("vector", lambda eng: eng.tensor_add(b2row[:, :], b2row[:, :], dbrow[:, :]), "DV")
        op("vector", lambda eng: eng.memset(carry[:, :], 0.0), "DV")
        op("vector", lambda eng: eng.memset(epsA[:, :], EPS_ATTN), "DV")
        op("vector", lambda eng: eng.memset(epsO[:, :], EPS_OUT), "DV")
        dv_init = S.cur("DV")

        wait("tensor", "DV", dv_init)
        bctmp = sb("bctmp", [128, DM], c=pctx)
        for row, dstd in ((grow, gbc_d), (brow, bbc_d), (b2row, b2bc_d)):
            for half in range(2):
                bank = "pC" if half == 0 else "pD"
                Pb = pC if half == 0 else pD
                pe_acquire(bank)
                pv = op("tensor", lambda eng, r=row, hf=half, P=Pb: eng.matmul(
                    P[:, :], ones1[:, :], r[:1, hf * 512:(hf + 1) * 512],
                    start=True, stop=True), "PE")
                wait("vector", "PE", pv)
                dv_ = op("vector", lambda eng, hf=half, P=Pb: eng.tensor_copy(
                    bctmp[:, hf * 512:(hf + 1) * 512], P[:, :]), "DV")
                bank_freed(bank, "DV", dv_)
            wait("gpsimd", "DV", S.cur("DV"))
            dma("gpsimd", dstd.ap(), bctmp[:, :], "GD")
            wait("vector", "GD", S.cur("GD"))

        def build_smat(col, vec_d, vec_o, s_i):
            op("vector", lambda eng, c=col, v=vec_d: eng.tensor_scalar_mul(
                smat[:, :], identf[:, :], v[:, c:c + 1]), "DV")
            if s_i <= 6:
                op("vector", lambda eng, c=col, v=vec_o, si=s_i: eng.tensor_scalar_mul(
                    smat2[:, :], permsS[:, si * 128:(si + 1) * 128], v[:, c:c + 1]), "DV")
                op("vector", lambda eng: eng.tensor_add(smat[:, :], smat[:, :], smat2[:, :]), "DV")
                return op("vector", lambda eng: eng.tensor_copy(smatb[:, :], smat[:, :]), "DV")
            else:
                op("vector", lambda eng, c=col, v=vec_o: eng.tensor_scalar_mul(
                    smat2[:, :], identf[:, :], v[:, c:c + 1]), "DV")
                op("vector", lambda eng: eng.tensor_copy(smatb[:, :], smat[:, :]), "DV")
                return op("vector", lambda eng: eng.tensor_copy(smatb2[:, :], smat2[:, :]), "DV")

        def mm_block(dstP, lhsT_t, src_off, start, stop, has_bias):
            for hf in range(2):
                op("tensor", lambda eng, P=dstP, L=lhsT_t, o=src_off + hf * 512,
                   O=hf * 512, st=start, sp=stop: eng.matmul(
                       P[:, O:O + 512], L[:, :], cbuf[:, o:o + 512], start=st, stop=sp), "PE")
            if has_bias:
                op("tensor", lambda eng, L=lhsT_t, o=src_off + DM, st=start, sp=stop:
                   eng.matmul(pC[:, 0:1], L[:, :], cbuf[:, o:o + 1], start=st, stop=sp), "PE")
            return S.cur("PE")

        def compose(wd_t, wo_t, wop_t, reverse, has_bias):
            order = list(range(11, -1, -1)) if reverse else list(range(12))
            for s_i in order:
                if s_i <= 6:
                    vo = wo_t if reverse else wop_t
                    for blk in range(HB):
                        col = s_i * HB + blk
                        v1 = build_smat(col, wd_t, vo, s_i)
                        wait("tensor", "DV", v1)
                        pe_acquire("pA")
                        if has_bias:
                            pe_acquire("pC")
                        pv = mm_block(pA, smatb, blk * BW, True, True, has_bias)
                        wait("vector", "PE", pv)
                        dva = op("vector", lambda eng, b=blk: eng.tensor_copy(
                            cbuf[:, b * BW: b * BW + DM], pA[:, :]), "DV")
                        if has_bias:
                            dvc = op("vector", lambda eng, b=blk: eng.tensor_copy(
                                cbuf[:, b * BW + DM: b * BW + DM + 1], pC[:, 0:1]), "DV")
                            bank_freed("pC", "DV", dvc)
                        bank_freed("pA", "DV", dva)
                        wait("vector", "PE", pv)
                else:
                    vo = wop_t if reverse else wo_t
                    for blk in range(HB):
                        partner = blk ^ (2 ** (s_i - 7))
                        if partner < blk:
                            continue
                        col = s_i * HB + blk
                        colp = s_i * HB + partner
                        v1 = build_smat(col, wd_t, vo, s_i)
                        wait("tensor", "DV", v1)
                        pe_acquire("pA")
                        if has_bias:
                            pe_acquire("pC")
                        mm_block(pA, smatb, blk * BW, True, False, has_bias)
                        mm_block(pA, smatb2, partner * BW, False, True, has_bias)
                        v2 = build_smat(colp, wd_t, vo, s_i)
                        wait("tensor", "DV", v2)
                        pe_acquire("pB")
                        if has_bias:
                            pe_acquire("pD")
                        for hf in range(2):
                            op("tensor", lambda eng, o=partner * BW + hf * 512, O=hf * 512:
                               eng.matmul(pB[:, O:O + 512], smatb[:, :], cbuf[:, o:o + 512],
                                          start=True, stop=False), "PE")
                        for hf in range(2):
                            op("tensor", lambda eng, o=blk * BW + hf * 512, O=hf * 512:
                               eng.matmul(pB[:, O:O + 512], smatb2[:, :], cbuf[:, o:o + 512],
                                          start=False, stop=True), "PE")
                        if has_bias:
                            op("tensor", lambda eng, o=partner * BW + DM: eng.matmul(
                                pD[:, 0:1], smatb[:, :], cbuf[:, o:o + 1],
                                start=True, stop=False), "PE")
                            op("tensor", lambda eng, o=blk * BW + DM: eng.matmul(
                                pD[:, 0:1], smatb2[:, :], cbuf[:, o:o + 1],
                                start=False, stop=True), "PE")
                        pv2 = S.cur("PE")
                        wait("vector", "PE", pv2)
                        dva = op("vector", lambda eng, b=blk: eng.tensor_copy(
                            cbuf[:, b * BW: b * BW + DM], pA[:, :]), "DV")
                        dvb = op("vector", lambda eng, b=partner: eng.tensor_copy(
                            cbuf[:, b * BW: b * BW + DM], pB[:, :]), "DV")
                        bank_freed("pA", "DV", dva)
                        bank_freed("pB", "DV", dvb)
                        if has_bias:
                            dvc = op("vector", lambda eng, b=blk: eng.tensor_copy(
                                cbuf[:, b * BW + DM: b * BW + DM + 1], pC[:, 0:1]), "DV")
                            dvd = op("vector", lambda eng, b=partner: eng.tensor_copy(
                                cbuf[:, b * BW + DM: b * BW + DM + 1], pD[:, 0:1]), "DV")
                            bank_freed("pC", "DV", dvc)
                            bank_freed("pD", "DV", dvd)
                        wait("vector", "PE", pv2)

        for kc in range(NKC):
            if kc > 0:
                wait("gpsimd", "PE", S.cur("PE"))
            gv = dma("gpsimd", wtmp[:, :], Wexp.ap()[kc * 128:(kc + 1) * 128, :], "GD")
            wait("tensor", "GD", gv)
            for hb in range(HB):
                par = hb % 2
                bank = "pE" if par == 0 else "pF"
                P_ = pE if par == 0 else pF
                pe_acquire(bank)
                pv = op("tensor", lambda eng, H=hb, P=P_: eng.transpose(
                    P[:, 0:128], wtmp[:, H * 128:(H + 1) * 128], identb[:, :]), "PE")
                wait("vector", "PE", pv)
                dv_ = op("vector", lambda eng, H=hb, K=kc, P=P_: eng.tensor_copy(
                    cbuf[:, H * BW + K * 128: H * BW + (K + 1) * 128], P[:, 0:128]), "DV")
                bank_freed(bank, "DV", dv_)
        for hb in range(HB):
            dma("gpsimd", cbuf[:, hb * BW + DM: hb * BW + DM + 1],
                bexp.ap()[0:1, hb * 128:(hb + 1) * 128].rearrange("a b -> b a"), "GD")
        gv = S.cur("GD")
        wait("vector", "GD", gv)
        dvp = op("vector", lambda eng: eng.memset(
            cbuf.ap().rearrange("p (hb w) -> p hb w", hb=HB, w=BW)[:, :, DM + 1:], 0.0), "DV")
        wait("tensor", "DV", dvp)
        compose(wdUS, woUS, woUPS, False, True)
        op("vector", lambda eng: eng.tensor_add(
            beff[:, :], beff[:, :],
            cbuf.ap().rearrange("p (hb w) -> p hb w", hb=HB, w=BW)[:, :, DM]), "DV")
        gd_stage_prev = 0
        for kc in range(NKC):
            if gd_stage_prev:
                wait("vector", "GD", gd_stage_prev)
            for hb in range(HB):
                par = hb % 2
                bank = "pE" if par == 0 else "pF"
                P_ = pE if par == 0 else pF
                pe_acquire(bank)
                pv = op("tensor", lambda eng, H=hb, K=kc, P=P_: eng.transpose(
                    P[:, 0:128], cbuf[:, H * BW + K * 128: H * BW + (K + 1) * 128],
                    identb[:, :]), "PE")
                wait("vector", "PE", pv)
                if kc > 0:
                    pass
                dv_ = op("vector", lambda eng, H=hb, P=P_: eng.tensor_copy(
                    wtmp[:, H * 128:(H + 1) * 128], P[:, 0:128]), "DV")
                bank_freed(bank, "DV", dv_)
            dvw = S.cur("DV")
            wait("gpsimd", "DV", dvw)
            gd_stage_prev = dma("gpsimd", wexp_d.ap()[:, kc * DHID:(kc + 1) * DHID], wtmp[:, :], "GD")
        gd_up = S.cur("GD")

        wait("gpsimd", "PE", S.cur("PE"))
        wait("gpsimd", "DV", S.cur("DV"))
        for hc in range(HB):
            dma("gpsimd", cbuf[:, hc * BW: hc * BW + DM],
                Wcon.ap()[hc * 128:(hc + 1) * 128, :], "GD")
        gv = S.cur("GD")
        wait("vector", "GD", gv)
        dvp2 = op("vector", lambda eng: eng.memset(
            cbuf.ap().rearrange("p (hb w) -> p hb w", hb=HB, w=BW)[:, :, DM:], 0.0), "DV")
        wait("tensor", "DV", dvp2)
        compose(wdDS, woDS, woDPS, True, False)
        wait("gpsimd", "DV", S.cur("DV"))
        for hc in range(HB):
            dma("gpsimd", wcon_d.ap()[:, hc * DM:(hc + 1) * DM],
                cbuf[:, hc * BW: hc * BW + DM], "GD")
        gd_dn = S.cur("GD")
        wait("gpsimd", "GD", gd_dn)
        wait("gpsimd", "PE", S.cur("PE"))
        wait("gpsimd", "DV", S.cur("DV"))
        if DEBUG:
            dma("gpsimd", d_wexp_pre.ap(), wexp_d.ap(), "GD")
            wait("gpsimd", "GD", S.cur("GD"))
        po_prep = op("gpsimd", lambda eng: eng.engine_nop(), "PO")

        pctx.close()

        fctx = ExitStack()
        xrb = sb("xrb", [128, DM], BF16, c=fctx)
        xfm = sb("xfm", [128, NKC * CH], BF16, c=fctx)
        xi = sb("xi", [128, G * (CH + 3)], c=fctx)
        xcv = sb("xcv", [128, G * CH], c=fctx)
        xcbP = sb("xcbP", [128, 2 * G * CH], BF16, c=fctx)
        gzbP = sb("gzbP", [128, 2 * G * CH], BF16, c=fctx)
        dtb = sb("dtb", [128, G * CH], c=fctx)
        dtmp = sb("dtmp", [128, CH], c=fctx)
        dtg = sb("dtg", [128, CH], c=fctx)
        vbf = sb("vbf", [128, G * CH], BF16, c=fctx)
        xdrO = sb("xdrO", [64, 2 * CH], c=fctx)
        xbcO = sb("xbcO", [32, 2 * CH], c=fctx)
        xdr = sb("xdr", [64, CH], c=fctx)
        xbcB = sb("xbcB", [32, CH], BF16, c=fctx)
        xdrQ = sb("xdrQ", [64, CH], c=fctx)
        xbcQ = sb("xbcQ", [32, CH], c=fctx)
        selS = sb("selS", [32, 32 * 128], BF16, c=fctx)
        dtrb = sb("dtrb", [64, CH], BF16, c=fctx)
        Bbc = sb("Bbc", [128, 8 * CH], BF16, c=fctx)
        Cbc = sb("Cbc", [128, 8 * CH], BF16, c=fctx)
        hG = sb("hG", [128, 2 * CH], BF16, c=fctx)
        yall = sb("yall", [128, G * CH], BF16, c=fctx)
        qt0 = sb("qt0", [128, CH], BF16, c=fctx)
        qt1 = sb("qt1", [128, CH], BF16, c=fctx)
        dAb = sb("dAb", [128, 4 * CH], BF16, c=fctx)
        dBb = sb("dBb", [128, 4 * CH], BF16, c=fctx)
        gt = sb("gt", [128, 2 * CH], BF16, c=fctx)

        wait("vector", "PO", po_prep)
        wait("scalar", "PO", po_prep)
        wait("tensor", "PO", po_prep)
        wait("sync", "PO", po_prep)
        wait("gpsimd", "PO", po_prep)
        dv_xi0 = op("vector", lambda eng: eng.memset(xi[:, :], 0.0), "DV")
        wait("scalar", "DV", dv_xi0)
        ls_sel = dma("sync", selS[:, :], selmat_in.ap(), "LD")
        wait("tensor", "LD", ls_sel)

        trk = {}
        ldx = {}
        win_pe = {}
        evac_st = {}
        tail_dv = {}
        conv_dv = {}
        conv_g_dv = {}
        silu_ac = {}
        silu_g_ac = {}
        dtb_g_dv = {}
        vbf_g_dv = {}
        xfm_ac = {}
        exp_last = {}
        pool_last = {}
        selpe_last = {}
        xdradd_dv = {}
        bbc_dv = {}
        dbu_last_slot = {}
        slot_dv = [0, 0, 0, 0]
        slot_ctr = [0]
        WBANK = [("pAa", 0, pA), ("pAb", 512, pA), ("pC", 0, pC), ("pD", 0, pD)]

        def emit_roles(colix):
            opn("vector", lambda eng, C=colix: eng.reg_load(regs["v_ro"], roleS[:1, C:C + 1]))
            opn("tensor", lambda eng, C=colix: eng.reg_load(regs["t_ro"], roleS[:1, C:C + 1]))
            opn("gpsimd", lambda eng, C=colix: eng.reg_load(regs["g_ro"], roleS[:1, C:C + 1]))

        def xload(k, sub):
            prev = trk.get((k, sub - 1) if sub > 0 else (k - 1, 3), 0)
            wait("gpsimd", "PE", prev)
            ldx[(k, sub)] = dma(
                "gpsimd", xrb[:, :],
                x_full.ap()[k * CH + sub * 128: k * CH + (sub + 1) * 128, :], "GD")

        def emit_transposes(k):
            if k > 0:
                wait("scalar", "PE", win_pe[(k - 1, 15)])
            for sub in range(4):
                wait("tensor", "GD", ldx[(k, sub)])
                for kc in range(NKC):
                    pb = "pE" if kc % 2 == 0 else "pF"
                    P_ = pE if kc % 2 == 0 else pF
                    pe_acquire(pb)
                    pv = op("tensor", lambda eng, K=kc, P=P_: eng.transpose(
                        P[:, 0:128], xrb[:, K * 128:(K + 1) * 128],
                        identb[:, :]), "PE")
                    wait("scalar", "PE", pv)
                    av = op("scalar", lambda eng, K=kc, Sb=sub, P=P_: eng.activation(
                        xfm[:, K * CH + Sb * 128: K * CH + Sb * 128 + 128],
                        P[:, 0:128], AF.Copy), "AC")
                    bank_freed(pb, "AC", av)
                trk[(k, sub)] = S.cur("PE")
                if sub < 3:
                    xload(k, sub + 1)
            xfm_ac[k] = S.cur("AC")

        def emit_win(k):
            wait("tensor", "AC", xfm_ac[k])
            for fb in range(16):
                bname, off, P_ = WBANK[fb % 4]
                if fb >= 4:
                    wait("tensor", "ST", evac_st[(k, fb - 4)])
                else:
                    pe_acquire(bname)
                for kc in range(NKC):
                    op("tensor", lambda eng, F=fb, K=kc, O=off, P=P_: eng.matmul(
                        P[:, O:O + CH], wiS[:, (F * NKC + K) * 128:(F * NKC + K + 1) * 128],
                        xfm[:, K * CH:(K + 1) * CH],
                        start=(K == 0), stop=(K == NKC - 1)), "PE")
                win_pe[(k, fb)] = S.cur("PE")

        def make_evacs(k):
            parA = k % 2
            tasks = []
            for fb in range(16):
                stv = S.inc("ST", 1)
                evac_st[(k, fb)] = stv

                def t(fb=fb, k=k, parA=parA, stv=stv):
                    bname, off, P_ = WBANK[fb % 4]
                    if fb == 0 and k > 0:
                        wait("scalar", "DV", tail_dv.get(k, 0))
                    wait("scalar", "PE", win_pe[(k, fb)])
                    if fb < 8:
                        E("scalar", lambda eng, F=fb, O=off, P=P_, s=sems["ST"]: eng.activation(
                            xi[:, F * (CH + 3) + 3: F * (CH + 3) + 3 + CH],
                            P[:, O:O + CH], AF.Copy).then_inc(s, 1))
                    else:
                        E("scalar", lambda eng, F=fb, O=off, P=P_, PA=parA, s=sems["ST"]:
                          eng.activation(
                              gzbP[:, PA * G * CH + (F - 8) * CH: PA * G * CH + (F - 7) * CH],
                              P[:, O:O + CH], AF.Silu).then_inc(s, 1))
                    bank_freed(bname, "ST", stv)
                tasks.append(t)
            return tasks

        def make_conv(k):
            tasks = []
            for g in range(G):
                def t(g=g, k=k):
                    base = g * (CH + 3)
                    wait("vector", "ST", evac_st[(k, g)])
                    if g == 0 and k > 0:
                        wait("vector", "AC", silu_ac.get(k - 1, 0))
                    op("vector", lambda eng, b=base, gg=g: eng.tensor_scalar(
                        xcv[:, gg * CH:(gg + 1) * CH], xi[:, b: b + CH],
                        convwS[:, gg * 4: gg * 4 + 1], convbS[:, gg: gg + 1],
                        ALU.mult, ALU.add), "DV")
                    for kk in range(1, 4):
                        op("vector", lambda eng, b=base, gg=g, K=kk: eng.scalar_tensor_tensor(
                            xcv[:, gg * CH:(gg + 1) * CH], xi[:, b + K: b + K + CH],
                            convwS[:, gg * 4 + K: gg * 4 + K + 1], xcv[:, gg * CH:(gg + 1) * CH],
                            ALU.mult, ALU.add), "DV")
                    conv_g_dv[(k, g)] = S.cur("DV")
                    if g == G - 1:
                        conv_dv[k] = S.cur("DV")
                tasks.append(t)
            return tasks

        def make_silus(k):
            parA = k % 2
            tasks = []
            for g in range(G):
                def t(g=g, k=k, parA=parA):
                    wait("scalar", "DV", conv_g_dv[(k, g)])
                    av = op("scalar", lambda eng, gg=g, PA=parA: eng.activation(
                        xcbP[:, PA * G * CH + gg * CH: PA * G * CH + (gg + 1) * CH],
                        xcv[:, gg * CH:(gg + 1) * CH], AF.Silu), "AC")
                    silu_g_ac[(k, g)] = av
                    if g == G - 1:
                        silu_ac[k] = av
                tasks.append(t)
            return tasks

        def emit_Bhead1(kB):
            poll(sh_flags.ap()[0, kB], sh_flags.ap()[1, kB])
            wait("gpsimd", "DV", xdradd_dv.get(kB - 1, 0))
            cdma_h("gpsimd", xdrQ[:, :], sh_xdbl.ap()[1, kB][0:64, :],
                   xdrQ[:, :], sh_xdbl.ap()[0, kB][0:64, :], "GD")
            gq = cdma_h("gpsimd", xbcQ[:, :], sh_xdbl.ap()[1, kB][64:96, :],
                        xbcQ[:, :], sh_xdbl.ap()[0, kB][64:96, :], "GD")
            parB = kB % 2
            wait("vector", "GD", gq)
            wait("vector", "PE", selpe_last.get(kB - 1, 0))
            op("vector", lambda eng, PB=parB: eng.tensor_add(
                xdr[:, :], xdrO[:, PB * CH:(PB + 1) * CH], xdrQ[:, :]), "DV")
            op("vector", lambda eng, PB=parB: eng.tensor_add(
                xbcB[:, :], xbcO[:, PB * CH:(PB + 1) * CH], xbcQ[:, :]), "DV")
            xdradd_dv[kB] = S.cur("DV")

        def emit_Bhead2(kB):
            parB = kB % 2
            dv_dtr = op("vector", lambda eng: eng.tensor_copy(dtrb[:, :], xdr[:, :]), "DV")
            wait("tensor", "DV", dv_dtr)
            wait("vector", "AC", exp_last.get(kB - 1, 0))
            for g in range(G):
                bank = "pC" if g % 2 == 0 else "pD"
                P_ = pC if g % 2 == 0 else pD
                pe_acquire(bank)
                pv = op("tensor", lambda eng, gg=g, P=P_: eng.matmul(
                    P[:, :], wdtS[:, gg * 128:(gg + 1) * 128], dtrb[:, :],
                    start=True, stop=True), "PE")
                wait("vector", "PE", pv)
                dvp = op("vector", lambda eng, P=P_, gg=g: eng.tensor_scalar_add(
                    dtg[:, :], P[:, :], bdtS[:, gg:gg + 1]), "DV")
                bank_freed(bank, "DV", dvp)
                wait("scalar", "DV", dvp)
                op("scalar", lambda eng: eng.activation(dtmp[:, :], dtg[:, :], AF.Abs), "AC")
                op("scalar", lambda eng: eng.activation(dtmp[:, :], dtmp[:, :], AF.Exp, scale=-1.0), "AC")
                acl = op("scalar", lambda eng: eng.activation(dtmp[:, :], dtmp[:, :], AF.Ln, bias=1.0), "AC")
                op("vector", lambda eng: eng.tensor_scalar_max(dtg[:, :], dtg[:, :], 0.0), "DV")
                wait("vector", "AC", acl)
                op("vector", lambda eng, gg=g: eng.tensor_add(
                    dtb[:, gg * CH:(gg + 1) * CH], dtg[:, :], dtmp[:, :]), "DV")
                dtb_g_dv[(kB, g)] = S.cur("DV")
            wait("vector", "PO", pool_last.get(kB - 1, 0))
            wait("vector", "AC", silu_ac[kB])
            for g in range(G):
                op("vector", lambda eng, gg=g, PB=parB: eng.tensor_mul(
                    vbf[:, gg * CH:(gg + 1) * CH], dtb[:, gg * CH:(gg + 1) * CH],
                    xcbP[:, PB * G * CH + gg * CH: PB * G * CH + (gg + 1) * CH]), "DV")
                vbf_g_dv[(kB, g)] = S.cur("DV")

        def emit_sel(kB, nq):
            for j in range(2):
                n = nq * 2 + j
                slot = (nq % 4) * 2 + j
                bank = "pBa" if j == 0 else "pBb"
                off = 0 if j == 0 else 512
                pe_acquire(bank)
                if nq == 0 and j == 0:
                    wait("tensor", "DV", xdradd_dv[kB])
                pv = op("tensor", lambda eng, N=n, O=off: eng.matmul(
                    pB[:, O:O + CH], selS[:, N * 128:(N + 1) * 128], xbcB[:, :],
                    start=True, stop=True), "PE")
                wait("vector", "PE", pv)
                gprev = dbu_last_slot.get(slot, 0)
                if gprev:
                    wait("vector", "PO", gprev)
                dv_ = op("vector", lambda eng, SL=slot, O=off: eng.tensor_copy(
                    Bbc[:, SL * CH:(SL + 1) * CH], pB[:, O:O + CH]), "DV")
                bank_freed(bank, "DV", dv_)
                bbc_dv[(kB, slot)] = dv_
            for j in range(2):
                n = nq * 2 + j
                slot = (nq % 4) * 2 + j
                bank = "pBa" if j == 0 else "pBb"
                off = 0 if j == 0 else 512
                pe_acquire(bank)
                pv = S.inc("PE", 1)
                g_if("tensor", "t_ro", 1)
                E("tensor", lambda eng, N=n, O=off, s=sems["PE"]: eng.matmul(
                    pB[:, O:O + CH], selS[:, (16 + N) * 128:(16 + N + 1) * 128], xbcB[:, :],
                    start=True, stop=True).then_inc(s, 1))
                g_else("tensor")
                balance("tensor", "PE", 1)
                g_end("tensor")
                dv_ = S.inc("DV", 1)
                g_if("vector", "v_ro", 1)
                E("vector", lambda eng, s=sems["PE"], v=pv: eng.wait_ge(s, v))
                E("vector", lambda eng, SL=slot, O=off, s=sems["DV"]: eng.tensor_copy(
                    Cbc[:, SL * CH:(SL + 1) * CH], pB[:, O:O + CH]).then_inc(s, 1))
                g_else("vector")
                balance("vector", "DV", 1)
                g_end("vector")
                bank_freed(bank, "DV", dv_)
            selpe_last[kB] = S.cur("PE")

        def emit_B2(kB, spr):
            pend = None

            def flush_pend():
                nonlocal pend
                if pend is None:
                    return
                g_, nq_, dvh = pend
                pend = None
                wait("gpsimd", "DV", dvh)
                poc = op("gpsimd", lambda eng, gg=g_, NQ=nq_: eng.tensor_copy(
                    carry[:, gg * NST + NQ * 2: gg * NST + NQ * 2 + 2],
                    hG.ap().rearrange("p (j t) -> p j t", j=2, t=CH)[:, :, CH - 1]), "PO")
                wait("vector", "PO", poc)

            for nq in range(8):
                if nq >= 4:
                    emit_sel(kB, nq)
                for g in range(G):
                    ms = {}
                    for j in range(2):
                        slot = (nq % 4) * 2 + j
                        m = slot_ctr[0] % 4
                        ms[j] = m
                        slot_ctr[0] += 1
                        if slot_dv[m]:
                            wait("gpsimd", "DV", slot_dv[m])
                        wait("gpsimd", "DV", bbc_dv[(kB, slot)])
                        pov = op("gpsimd", lambda eng, gg=g, SL=slot, M=m: eng.tensor_mul(
                            dBb[:, M * CH:(M + 1) * CH], vbf[:, gg * CH:(gg + 1) * CH],
                            Bbc[:, SL * CH:(SL + 1) * CH]), "PO")
                        ms[(j, "pov")] = pov
                        dbu_last_slot[slot] = pov
                    flush_pend()
                    for j in range(2):
                        n = nq * 2 + j
                        m = ms[j]
                        if slot_dv[m]:
                            wait("scalar", "DV", slot_dv[m])
                        if nq == 0:
                            wait("scalar", "DV", dtb_g_dv[(kB, g)])
                        aca = op("scalar", lambda eng, gg=g, N=n, M=m: eng.activation(
                            dAb[:, M * CH:(M + 1) * CH], dtb[:, gg * CH:(gg + 1) * CH],
                            AF.Exp, scale=Atile[:, gg * NST + N: gg * NST + N + 1]), "AC")
                        wait("vector", "AC", aca)
                        wait("vector", "PO", ms[(j, "pov")])
                        dvs = op("vector", lambda eng, gg=g, N=n, J=j, M=m: eng.tensor_tensor_scan(
                            hG[:, J * CH:(J + 1) * CH], dAb[:, M * CH:(M + 1) * CH],
                            dBb[:, M * CH:(M + 1) * CH],
                            carry[:, gg * NST + N: gg * NST + N + 1],
                            ALU.mult, ALU.add), "DV")
                        slot_dv[m] = dvs
                    sl0 = (nq % 4) * 2
                    S.inc("DV", 1)
                    g_if("vector", "v_ro", 1)
                    E("vector", lambda eng, SL=sl0: eng.tensor_mul(
                        qt0[:, :], hG[:, 0:CH], Cbc[:, SL * CH:(SL + 1) * CH]))
                    E("vector", lambda eng, SL=sl0 + 1: eng.tensor_mul(
                        qt1[:, :], hG[:, CH:2 * CH], Cbc[:, SL * CH:(SL + 1) * CH]))
                    E("vector", lambda eng: eng.tensor_add(qt0[:, :], qt0[:, :], qt1[:, :]))
                    if nq == 0:
                        E("vector", lambda eng, gg=g, s=sems["DV"]: eng.tensor_copy(
                            yall[:, gg * CH:(gg + 1) * CH], qt0[:, :]).then_inc(s, 1))
                    else:
                        E("vector", lambda eng, gg=g, s=sems["DV"]: eng.tensor_add(
                            yall[:, gg * CH:(gg + 1) * CH], yall[:, gg * CH:(gg + 1) * CH],
                            qt0[:, :]).then_inc(s, 1))
                    g_else("vector")
                    balance("vector", "DV", 1)
                    g_end("vector")
                    pend = (g, nq, S.cur("DV"))
                for t in spr.get(nq, []):
                    t()
            flush_pend()
            exp_last[kB] = S.cur("AC")
            pool_last[kB] = S.cur("PO")

        def emit_Atail(k):
            parA = k % 2
            pe_acquire("pC")
            pe_acquire("pD")
            pv = 0
            for g in range(G):
                wait("tensor", "AC", silu_g_ac[(k, g)])
                op("tensor", lambda eng, gg=g, PA=parA: eng.matmul(
                    pC[0:64, :], wxpS[:, gg * 96: gg * 96 + 64],
                    xcbP[:, PA * G * CH + gg * CH: PA * G * CH + (gg + 1) * CH],
                    start=(gg == 0), stop=(gg == G - 1)), "PE")
                pv = op("tensor", lambda eng, gg=g, PA=parA: eng.matmul(
                    pD[0:32, :], wxpS[:, gg * 96 + 64: gg * 96 + 96],
                    xcbP[:, PA * G * CH + gg * CH: PA * G * CH + (gg + 1) * CH],
                    start=(gg == 0), stop=(gg == G - 1)), "PE")
            wait("vector", "PE", pv)
            d1 = op("vector", lambda eng, PA=parA: eng.tensor_copy(
                xdrO[:, PA * CH:(PA + 1) * CH], pC[0:64, :]), "DV")
            d2 = op("vector", lambda eng, PA=parA: eng.tensor_copy(
                xbcO[:, PA * CH:(PA + 1) * CH], pD[0:32, :]), "DV")
            bank_freed("pC", "DV", d1)
            bank_freed("pD", "DV", d2)
            wait("gpsimd", "DV", d2)
            cdma_h("gpsimd", sh_xdbl.ap()[0, k][0:64, :], xdrO[:, parA * CH:(parA + 1) * CH],
                   sh_xdbl.ap()[1, k][0:64, :], xdrO[:, parA * CH:(parA + 1) * CH], "GD")
            gdp = cdma_h("gpsimd", sh_xdbl.ap()[0, k][64:96, :], xbcO[:, parA * CH:(parA + 1) * CH],
                         sh_xdbl.ap()[1, k][64:96, :], xbcO[:, parA * CH:(parA + 1) * CH], "GD")
            wait("gpsimd", "GD", gdp)
            cdma_h("gpsimd", sh_flags.ap()[0, k], magicS[:, :],
                   sh_flags.ap()[1, k], magicS[:, :], "GD")

        def emit_B3(kB):
            parB = kB % 2
            wait("vector", "ST", evac_st[(kB, 15)])
            gsp_hist = {}
            for g in range(G):
                dvg = S.inc("DV", 1)
                g_if("vector", "v_ro", 1)
                if g >= 2:
                    E("vector", lambda eng, s=sems["GD"], v=gsp_hist[g - 2]: eng.wait_ge(s, v))
                E("vector", lambda eng, gg=g, PB=parB: eng.scalar_tensor_tensor(
                    dtg[:, :], xcbP[:, PB * G * CH + gg * CH: PB * G * CH + (gg + 1) * CH],
                    DpS[:, gg:gg + 1], yall[:, gg * CH:(gg + 1) * CH], ALU.mult, ALU.add))
                E("vector", lambda eng, gg=g, PB=parB, s=sems["DV"]: eng.tensor_mul(
                    gt[:, (gg % 2) * CH:(gg % 2 + 1) * CH], dtg[:, :],
                    gzbP[:, PB * G * CH + gg * CH: PB * G * CH + (gg + 1) * CH]).then_inc(s, 1))
                g_else("vector")
                balance("vector", "DV", 1)
                g_end("vector")
                gsp = S.inc("GD", 16)
                gsp_hist[g] = gsp
                kl = kB % 16
                g_if("gpsimd", "g_ro", 1)
                E("gpsimd", lambda eng, v=dvg, s=sems["DV"]: eng.wait_ge(s, v))
                g_if("gpsimd", "g_h", 0)
                E("gpsimd", lambda eng, gg=g, KL=kl, s=sems["GD"]: eng.dma_start(
                    out=sh_gated.ap()[gg * 128:(gg + 1) * 128, KL * CH:(KL + 1) * CH],
                    in_=gt[:, (gg % 2) * CH:(gg % 2 + 1) * CH]).then_inc(s, 16))
                g_else("gpsimd")
                E("gpsimd", lambda eng, gg=g, KL=kl, s=sems["GD"]: eng.dma_start(
                    out=sh_gated.ap()[1024 + gg * 128: 1024 + (gg + 1) * 128, KL * CH:(KL + 1) * CH],
                    in_=gt[:, (gg % 2) * CH:(gg % 2 + 1) * CH]).then_inc(s, 16))
                g_end("gpsimd")
                g_else("gpsimd")
                balance("gpsimd", "GD", 16)
                g_end("gpsimd")
            g_if("vector", "v_ro", 1)
            E("vector", lambda eng, s=sems["GD"], v=gsp_hist[G - 1]: eng.wait_ge(s, v))
            g_else("vector")
            g_end("vector")

        for step in range(NCHUNK + 1):
            kA = step if step < NCHUNK else None
            kB = step - 1 if step >= 1 else None
            if kB is not None and kB in (0, 16):
                emit_roles(1 if kB == 0 else 2)
            if kA is not None:
                xload(kA, 0)
            if kB is not None:
                emit_Bhead1(kB)
            if kA is not None:
                emit_transposes(kA)
            if kB is not None:
                emit_Bhead2(kB)
            if kA is not None and kA > 0:
                for g in range(G):
                    op("vector", lambda eng, gg=g: eng.tensor_copy(
                        xi[:, gg * (CH + 3): gg * (CH + 3) + 3],
                        xi[:, gg * (CH + 3) + CH: gg * (CH + 3) + CH + 3]), "DV")
                tail_dv[kA] = S.cur("DV")
            if kB is not None:
                for nq in range(4):
                    emit_sel(kB, nq)
            if kA is not None:
                evq = make_evacs(kA)
                cvq = make_conv(kA)
                svq = make_silus(kA)
                emit_win(kA)
            else:
                evq, cvq, svq = [], [], []
            if kB is not None:
                spr = {0: evq[0:4], 1: evq[4:8],
                       2: evq[8:12] + cvq[0:2],
                       3: evq[12:16] + cvq[2:4] + svq[0:2],
                       4: cvq[4:6] + svq[2:4],
                       5: cvq[6:8] + svq[4:6],
                       6: svq[6:8], 7: []}
                emit_B2(kB, spr)
            else:
                for t in evq + cvq + svq:
                    t()
            if kA is not None:
                emit_Atail(kA)
            if kB is not None:
                emit_B3(kB)

        gall = S.cur("GD")
        wait("gpsimd", "GD", gall)
        cdma_h("gpsimd", sh_flags.ap()[0, 40], magicS[:, :],
               sh_flags.ap()[1, 40], magicS[:, :], "GD")
        poll(sh_flags.ap()[0, 40], sh_flags.ap()[1, 40])
        po_rdy = op("gpsimd", lambda eng: eng.engine_nop(), "PO")
        wait("sync", "PO", po_rdy)
        wait("tensor", "PO", po_rdy)
        wait("vector", "PO", po_rdy)
        wait("scalar", "PO", po_rdy)
        if TESTDELAY:
            for _ in range(2000):
                op("vector", lambda eng: eng.memset(dtmp[:, :], 0.0), "DV")
            wait("sync", "DV", S.cur("DV"))
            wait("gpsimd", "DV", S.cur("DV"))
            wait("tensor", "DV", S.cur("DV"))
            wait("scalar", "DV", S.cur("DV"))
        fctx.close()
        mctx.close()

        b0 = ExitStack()
        gbc = sb("gbc", [128, DM], c=b0)
        bbc = sb("bbc", [128, DM], c=b0)
        ld_bc = dma("sync", gbc[:, :], gbc_d.ap(), "LD")
        ld_bc = dma("sync", bbc[:, :], bbc_d.ap(), "LD")
        wait("vector", "LD", ld_bc)
        woutS = sb("woutS", [128, 16 * DM], BF16, c=b0)
        gld = sb("gld", [128, 16 * 128], BF16, c=b0)
        mt = sb("mt", [128, DM], c=b0)
        fence = sb("fence", [1, 16], BF16, c=b0)
        stat = sb("stat", [128, 8], c=b0)
        lnt = sb("lnt", [128, DM], c=b0)
        lnu = sb("lnu", [128, DM], c=b0)
        x1 = sb("x1", [128, DM], c=b0)
        xq = sb("xq", [128, DM], c=b0)

        wait("gpsimd", "PE", S.cur("PE"))
        wait("gpsimd", "AC", S.cur("AC"))
        for dch in range(2):
            gw = dma("gpsimd", woutS[:, dch * 8 * DM:(dch + 1) * 8 * DM],
                     bass.AP(Wout, dch * 8 * 128 * 1024,
                             [[1024, 128], [128 * 1024, 8], [1, 1024]]), "GD")
        wait("tensor", "GD", gw)
        prev_pv_b0 = 0
        prev_dvx1_b0 = 0
        for tt in range(32):
            if tt > 0:
                wait("sync", "PE", prev_pv_b0)
                wait("sync", "DV", prev_dvx1_b0)
            lv = cdma_h(
                "sync", gld[:, :],
                bass.AP(sh_gated, tt * 128, [[8192, 128], [128 * 8192, 16], [1, 128]]),
                gld[:, :],
                bass.AP(sh_gated, 4096 + tt * 128, [[8192, 128], [128 * 8192, 16], [1, 128]]),
                "LD")
            lx = dma("sync", xq[:, :], x_own.ap()[tt * 128:(tt + 1) * 128, :], "LD")
            lvf = dma("sync", fence[:, :], gld[0:1, 2032:2048], "LD")
            wait("tensor", "LD", lvf)
            par = tt % 2
            Pm = pA if par == 0 else pB
            nm = "pA" if par == 0 else "pB"
            pe_acquire(nm, nm + "a", nm + "b")
            for dc in range(16):
                for hf in range(2):
                    op("tensor", lambda eng, D=dc, P=Pm, O=hf * 512: eng.matmul(
                        P[:, O:O + 512], gld[:, D * 128:(D + 1) * 128],
                        woutS[:, D * DM + O: D * DM + O + 512],
                        start=(D == 0), stop=(D == 15)), "PE")
            pv = S.cur("PE")
            wait("vector", "PE", pv)
            dv_m = op("vector", lambda eng, P=Pm: eng.tensor_copy(mt[:, :], P[:, :]), "DV")
            bank_freed(nm, "DV", dv_m)
            bank_freed(nm + "a", "DV", dv_m)
            bank_freed(nm + "b", "DV", dv_m)
            if DEBUG:
                wait("gpsimd", "DV", dv_m)
                gm_ = dma("gpsimd", d_m.ap()[tt * 128:(tt + 1) * 128, :], mt[:, :], "GD")
                if tt == 0:
                    dma("gpsimd", d_gld.ap(), gld[:, :], "GD")
            opd("vector", lambda eng: eng.reduce_sum(stat[:, 0:1], mt[:, :], axis=AX.X))
            wait("scalar", "DV", dv_m)
            acq = op("scalar", lambda eng: eng.activation(lnt[:, :], mt[:, :], AF.Square), "AC")
            wait("vector", "AC", acq)
            opd("vector", lambda eng: eng.reduce_sum(stat[:, 1:2], lnt[:, :], axis=AX.X))
            opd("vector", lambda eng: eng.tensor_scalar_mul(stat[:, 2:3], stat[:, 0:1], 1.0 / DM))
            opd("vector", lambda eng: eng.tensor_scalar_mul(stat[:, 3:4], stat[:, 1:2], 1.0 / DM))
            opd("vector", lambda eng: eng.tensor_mul(stat[:, 4:5], stat[:, 2:3], stat[:, 2:3]))
            dvv = opd("vector", lambda eng: eng.tensor_sub(stat[:, 4:5], stat[:, 3:4], stat[:, 4:5]))
            wait("scalar", "DV", dvv)
            acs = op("scalar", lambda eng: eng.activation(
                stat[:, 5:6], stat[:, 4:5], AF.Sqrt, bias=epsA[:, :]), "AC")
            wait("vector", "AC", acs)
            opd("vector", lambda eng: eng.reciprocal(stat[:, 6:7], stat[:, 5:6]))
            op("vector", lambda eng: eng.tensor_scalar(
                lnt[:, :], mt[:, :], stat[:, 2:3], stat[:, 6:7], ALU.subtract, ALU.mult), "DV")
            op("vector", lambda eng: eng.tensor_mul(lnt[:, :], lnt[:, :], gbc[:, :]), "DV")
            dvlt = op("vector", lambda eng: eng.tensor_add(lnt[:, :], lnt[:, :], bbc[:, :]), "DV")
            wait("vector", "LD", lx)
            op("vector", lambda eng: eng.scalar_tensor_tensor(
                lnu[:, :], xq[:, :], 2.0, lnt[:, :], ALU.mult, ALU.add), "DV")
            dvu = S.cur("DV")
            opd("vector", lambda eng: eng.reduce_sum(stat[:, 0:1], lnu[:, :], axis=AX.X))
            wait("scalar", "DV", dvu)
            if DEBUG:
                wait("scalar", "GD", S.cur("GD"))
            acq2 = op("scalar", lambda eng: eng.activation(mt[:, :], lnu[:, :], AF.Square), "AC")
            wait("vector", "AC", acq2)
            opd("vector", lambda eng: eng.reduce_sum(stat[:, 1:2], mt[:, :], axis=AX.X))
            opd("vector", lambda eng: eng.tensor_scalar_mul(stat[:, 2:3], stat[:, 0:1], 1.0 / DM))
            opd("vector", lambda eng: eng.tensor_scalar_mul(stat[:, 3:4], stat[:, 1:2], 1.0 / DM))
            opd("vector", lambda eng: eng.tensor_mul(stat[:, 4:5], stat[:, 2:3], stat[:, 2:3]))
            dvv2 = opd("vector", lambda eng: eng.tensor_sub(stat[:, 4:5], stat[:, 3:4], stat[:, 4:5]))
            wait("scalar", "DV", dvv2)
            acs2 = op("scalar", lambda eng: eng.activation(
                stat[:, 5:6], stat[:, 4:5], AF.Sqrt, bias=epsO[:, :]), "AC")
            wait("vector", "AC", acs2)
            opd("vector", lambda eng: eng.reciprocal(stat[:, 6:7], stat[:, 5:6]))
            dvx1 = op("vector", lambda eng: eng.tensor_scalar(
                x1[:, :], lnu[:, :], stat[:, 2:3], stat[:, 6:7], ALU.subtract, ALU.mult), "DV")
            if DEBUG:
                wait("gpsimd", "DV", dvx1)
                dma("gpsimd", d_stat.ap()[tt * 128:(tt + 1) * 128, :], stat[:, :], "GD")
                gl2 = dma("gpsimd", d_lnu.ap()[tt * 128:(tt + 1) * 128, :], lnu[:, :], "GD")
                wait("vector", "GD", gl2)
            wait("gpsimd", "DV", dvx1)
            gx = dma("gpsimd", x1_d.ap()[tt * 128:(tt + 1) * 128, :], x1[:, :], "GD")
            wait("vector", "GD", gx)
            gx_last_b0 = gx
            prev_pv_b0 = pv
            prev_dvx1_b0 = dvx1
        b0.close()

        b1 = ExitStack()
        b2bc = sb("b2bc", [128, DM], c=b1)
        wexpS = sb("wexpS", [128, NKC * DHID], BF16, c=b1)
        wconS = sb("wconS", [128, HB * DM], BF16, c=b1)
        x1t = sb("x1t", [128, 2 * DM], c=b1)
        x1b = sb("x1b", [128, DM], BF16, c=b1)
        x1fm = sb("x1fm", [128, NKC * FT], BF16, c=b1)
        gelt = sb("gelt", [128, HB * FT], BF16, c=b1)
        mt2 = sb("mt2", [128, DM], c=b1)
        gsc = sb("gsc", [128, FT], c=b1)
        stat2 = sb("stat2", [128, 8], c=b1)
        lnu2 = sb("lnu2", [128, DM], c=b1)
        outt = mt2

        wait("sync", "GD", gx_last_b0)
        ld_b2 = dma("sync", b2bc[:, :], b2bc_d.ap(), "LD")
        wait("vector", "LD", ld_b2)
        for qq in range(4):
            lw1 = dma("sync", wexpS[:, qq * 8192:(qq + 1) * 8192],
                      wexp_d.ap()[:, qq * 8192:(qq + 1) * 8192], "LD")
        for qq in range(4):
            lw2 = dma("sync", wconS[:, qq * 8192:(qq + 1) * 8192],
                      wcon_d.ap()[:, qq * 8192:(qq + 1) * 8192], "LD")
        wait("tensor", "LD", lw2)
        if DEBUG:
            wait("gpsimd", "LD", S.cur("LD"))
            gearly = dma("gpsimd", d_wexpS2.ap(), wexpS[:, :], "GD")
            wait("sync", "GD", gearly)
        prev_dvo_b1 = 0
        for ft in range(QTOK // FT):
            if ft > 0:
                wait("sync", "DV", prev_dvo_b1)
            for sub in range(2):
                lx1 = dma("sync", x1t[:, sub * DM:(sub + 1) * DM],
                          x1_d.ap()[ft * FT + sub * 128: ft * FT + (sub + 1) * 128, :], "LD")
                wait("vector", "LD", lx1)
                dvb_ = op("vector", lambda eng, Sb=sub: eng.tensor_copy(
                    x1b[:, :], x1t[:, Sb * DM:(Sb + 1) * DM]), "DV")
                wait("tensor", "DV", dvb_)
                for kc in range(NKC):
                    par = kc % 2
                    bank = "pE" if par == 0 else "pF"
                    P_ = pE if par == 0 else pF
                    pe_acquire(bank)
                    pv = op("tensor", lambda eng, K=kc, P=P_: eng.transpose(
                        P[:, 0:128], x1b[:, K * 128:(K + 1) * 128], identb[:, :]), "PE")
                    wait("vector", "PE", pv)
                    dv_ = op("vector", lambda eng, K=kc, Sb=sub, P=P_: eng.tensor_copy(
                        x1fm[:, K * FT + Sb * 128: K * FT + Sb * 128 + 128], P[:, 0:128]), "DV")
                    bank_freed(bank, "DV", dv_)
                wait("vector", "PE", S.cur("PE"))
            if DEBUG and ft == 0:
                wait("gpsimd", "DV", S.cur("DV"))
                gdbg3 = dma("gpsimd", d_x1fm.ap(), x1fm[:, :], "GD")
                gdbg4 = dma("gpsimd", d_x1t.ap(), x1t[:, :], "GD")
                wait("gpsimd", "LD", S.cur("LD"))
                gdbg5 = dma("gpsimd", d_wexpS.ap(), wexpS[:, :], "GD")
                wait("vector", "GD", gdbg5)
                wait("sync", "GD", gdbg5)
                wait("tensor", "GD", gdbg5)
            wait("tensor", "DV", S.cur("DV"))
            for hb in range(HB):
                par = hb % 2
                bank = "pC" if par == 0 else "pD"
                Ph = pC if par == 0 else pD
                pe_acquire(bank)
                for kc in range(NKC):
                    op("tensor", lambda eng, H=hb, K=kc, P=Ph: eng.matmul(
                        P[:, 0:FT], wexpS[:, K * DHID + H * 128: K * DHID + (H + 1) * 128],
                        x1fm[:, K * FT:(K + 1) * FT],
                        start=(K == 0), stop=(K == NKC - 1)), "PE")
                pv = S.cur("PE")
                wait("scalar", "PE", pv)
                if hb > 0:
                    wait("scalar", "DV", dv_gc)
                if DEBUG and ft == 0 and hb == 0:
                    dvhp = op("vector", lambda eng, P=Ph: eng.tensor_copy(
                        mt2[:, 0:FT], P[:, 0:FT]), "DV")
                    wait("gpsimd", "DV", dvhp)
                    ghp = dma("gpsimd", d_hp.ap(), mt2[:, 0:FT], "GD")
                    wait("scalar", "GD", ghp)
                acg = op("scalar", lambda eng, H=hb, P=Ph: eng.activation(
                    gsc[:, :], P[:, 0:FT], AF.Gelu, bias=beff[:, H:H + 1]), "AC")
                bank_freed(bank, "AC", acg)
                wait("vector", "AC", acg)
                dv_gc = op("vector", lambda eng, H=hb: eng.tensor_copy(
                    gelt[:, H * FT:(H + 1) * FT], gsc[:, :]), "DV")
            if DEBUG and ft == 0:
                wait("gpsimd", "DV", S.cur("DV"))
                gdbg2 = dma("gpsimd", d_gelt.ap(), gelt[:, :], "GD")
                wait("vector", "GD", gdbg2)
                wait("scalar", "GD", gdbg2)
            wait("tensor", "AC", S.cur("AC"))
            wait("tensor", "DV", S.cur("DV"))
            for s2 in range(2):
                par = s2 % 2
                Py = pA if par == 0 else pB
                nm = "pA" if par == 0 else "pB"
                pe_acquire(nm, nm + "a", nm + "b")
                for hc in range(HB):
                    for hf in range(2):
                        op("tensor", lambda eng, H=hc, S2=s2, P=Py, O=hf * 512: eng.matmul(
                            P[:, O:O + 512], gelt[:, H * FT + S2 * 128: H * FT + S2 * 128 + 128],
                            wconS[:, H * DM + O: H * DM + O + 512],
                            start=(H == 0), stop=(H == HB - 1)), "PE")
                pvy = S.cur("PE")
                wait("vector", "PE", pvy)
                if DEBUG and ft == 0 and s2 == 0:
                    dvyp = op("vector", lambda eng, P=Py: eng.tensor_copy(mt2[:, :], P[:, :]), "DV")
                    wait("gpsimd", "DV", dvyp)
                    gyp = dma("gpsimd", d_yp.ap(), mt2[:, :], "GD")
                    wait("vector", "GD", gyp)
                    wait("scalar", "GD", gyp)
                op("vector", lambda eng, S2=s2, P=Py: eng.tensor_add(
                    lnu2[:, :], P[:, :], x1t[:, S2 * DM:(S2 + 1) * DM]), "DV")
                dvu2 = op("vector", lambda eng: eng.tensor_add(lnu2[:, :], lnu2[:, :], b2bc[:, :]), "DV")
                if DEBUG and ft == 0:
                    wait("gpsimd", "DV", dvu2)
                    gdbg = dma("gpsimd", d_lnu2.ap()[s2 * 128:(s2 + 1) * 128, :], lnu2[:, :], "GD")
                    wait("vector", "GD", gdbg)
                bank_freed(nm, "DV", dvu2)
                bank_freed(nm + "a", "DV", dvu2)
                bank_freed(nm + "b", "DV", dvu2)
                opd("vector", lambda eng: eng.reduce_sum(stat2[:, 0:1], lnu2[:, :], axis=AX.X))
                wait("scalar", "DV", dvu2)
                wait("scalar", "ST", S.cur("ST"))
                ac3 = op("scalar", lambda eng: eng.activation(mt2[:, :], lnu2[:, :], AF.Square), "AC")
                wait("vector", "AC", ac3)
                opd("vector", lambda eng: eng.reduce_sum(stat2[:, 1:2], mt2[:, :], axis=AX.X))
                opd("vector", lambda eng: eng.tensor_scalar_mul(stat2[:, 2:3], stat2[:, 0:1], 1.0 / DM))
                opd("vector", lambda eng: eng.tensor_scalar_mul(stat2[:, 3:4], stat2[:, 1:2], 1.0 / DM))
                opd("vector", lambda eng: eng.tensor_mul(stat2[:, 4:5], stat2[:, 2:3], stat2[:, 2:3]))
                dvv3 = opd("vector", lambda eng: eng.tensor_sub(stat2[:, 4:5], stat2[:, 3:4], stat2[:, 4:5]))
                wait("scalar", "DV", dvv3)
                ac4 = op("scalar", lambda eng: eng.activation(
                    stat2[:, 5:6], stat2[:, 4:5], AF.Sqrt, bias=epsO[:, :]), "AC")
                wait("vector", "AC", ac4)
                opd("vector", lambda eng: eng.reciprocal(stat2[:, 6:7], stat2[:, 5:6]))
                dvo = op("vector", lambda eng: eng.tensor_scalar(
                    outt[:, :], lnu2[:, :], stat2[:, 2:3], stat2[:, 6:7],
                    ALU.subtract, ALU.mult), "DV")
                wait("sync", "DV", dvo)
                stv = dma("sync", mout.ap()[ft * FT + s2 * 128: ft * FT + (s2 + 1) * 128, :],
                          outt[:, :], "ST")
                wait("vector", "ST", stv)
                prev_dvo_b1 = dvo
        b1.close()

        if DEBUG or DEBUG2 or DEBUG3:
            wait("gpsimd", "ST", S.cur("ST"))
            dma("gpsimd", d_beff.ap(), beff[:, :], "ST")
            dma("gpsimd", d_wexp.ap(), wexp_d.ap(), "ST")
            dma("gpsimd", d_wcon.ap(), wcon_d.ap(), "ST")
            dma("gpsimd", d_x1.ap(), x1_d.ap(), "ST")
            dma("gpsimd", d_gated.ap(), sh_gated.ap(), "ST")
            dma("gpsimd", d_xdbl.ap(),
                bass.AP(sh_xdbl, 0, [[CH, 2 * NCHUNK * 96], [1, CH]]), "ST")
        st_all = S.cur("ST")
        wait("gpsimd", "ST", st_all)

        with nc.Block() as block:
            @block.sync
            def _(eng):
                for fn in S.ops["sync"]:
                    fn(eng)

            @block.scalar
            def _(eng):
                for fn in S.ops["scalar"]:
                    fn(eng)

            @block.vector
            def _(eng):
                for fn in S.ops["vector"]:
                    fn(eng)

            @block.tensor
            def _(eng):
                for fn in S.ops["tensor"]:
                    fn(eng)

            @block.gpsimd
            def _(eng):
                for fn in S.ops["gpsimd"]:
                    fn(eng)

    return nc


_NC_CACHE = None


def kernel(**inputs):
    global _NC_CACHE
    f32 = np.float32

    x = np.ascontiguousarray(inputs["L0_attn_x"], f32)
    W_in = np.asarray(inputs["L0_attn_W_in"], f32)
    conv_w = np.asarray(inputs["L0_attn_conv_w"], f32)
    conv_b = np.asarray(inputs["L0_attn_conv_b"], f32)
    W_xproj = np.asarray(inputs["L0_attn_W_xproj"], f32)
    W_dt = np.asarray(inputs["L0_attn_W_dt"], f32)
    b_dt = np.asarray(inputs["L0_attn_b_dt"], f32)
    A_log = np.asarray(inputs["L0_attn_A_log"], f32)
    D_param = np.asarray(inputs["L0_attn_D_param"], f32)
    W_out = np.asarray(inputs["L0_attn_W_out"], f32)
    gamma = np.asarray(inputs["L0_attn_gamma"], f32)
    beta = np.asarray(inputs["L0_attn_beta"], f32)
    W_exp = np.asarray(inputs["L0_ffn_W_exp"], f32)
    b_exp = np.asarray(inputs["L0_ffn_b_exp"], f32)
    up_w = np.asarray(inputs["L0_ffn_up_weights"], f32)
    up_bias = np.asarray(inputs["L0_ffn_up_bias"], f32)
    dn_w = np.asarray(inputs["L0_ffn_down_weights"], f32)
    W_con = np.asarray(inputs["L0_ffn_W_con"], f32)
    b_con = np.asarray(inputs["L0_ffn_b_con"], f32)
    down_bias = np.asarray(inputs["L0_ffn_down_bias"], f32)

    wdU_, woU_ = _stage_vectors(up_w)
    wdD_, woD_ = _stage_vectors(dn_w)
    woUp_ = _perm(woU_)
    woDp_ = _perm(woD_)

    identb = np.eye(128, dtype=ml_dtypes.bfloat16)
    identf = np.eye(128, dtype=f32)
    perms = np.concatenate(
        [np.eye(128, dtype=f32)[:, np.arange(128) ^ (2 ** s)] for s in range(7)], axis=1)
    ones1 = np.ones((1, 128), f32)
    nonce = np.full((1, 16), np.random.randint(1, 2**31 - 1), np.int32)
    selmat = np.zeros((32, 32 * 128), ml_dtypes.bfloat16)
    for n in range(32):
        selmat[n, n * 128:(n + 1) * 128] = 1.0

    if _NC_CACHE is None:
        _NC_CACHE = build_program()
    nc = _NC_CACHE

    in_maps = []
    for c in range(8):
        b = c // 4
        p = (c // 2) % 2
        h = c % 2
        dlo = h * 1024
        qlo = p * 8192 + h * 4096
        xi_cols = W_in[:, dlo:dlo + 1024]
        z_cols = W_in[:, 2048 + dlo: 2048 + dlo + 1024]

        def colpack(vec):
            return np.ascontiguousarray(vec.reshape(G, 128).T, f32)

        m = {
            "x_full": np.ascontiguousarray(x[b]),
            "x_own": np.ascontiguousarray(x[b, qlo:qlo + QTOK]),
            "Wi": np.ascontiguousarray(np.concatenate([xi_cols, z_cols], axis=1)),
            "convw": np.ascontiguousarray(
                conv_w[dlo:dlo + 1024].reshape(G, 128, 4).transpose(1, 0, 2).reshape(128, G * 4)),
            "convb": colpack(conv_b[dlo:dlo + 1024]),
            "Wxp": np.ascontiguousarray(W_xproj[dlo:dlo + 1024]),
            "Wdt": np.ascontiguousarray(W_dt[:, dlo:dlo + 1024]),
            "bdt": colpack(b_dt[dlo:dlo + 1024]),
            "Alog": np.ascontiguousarray(
                A_log[dlo:dlo + 1024].reshape(G, 128, NST).transpose(1, 0, 2).reshape(128, G * NST)),
            "Dp": colpack(D_param[dlo:dlo + 1024]),
            "Wout": np.ascontiguousarray(W_out),
            "gamma": gamma.reshape(1, -1),
            "beta": beta.reshape(1, -1),
            "Wexp": W_exp,
            "bexp": b_exp.reshape(1, -1),
            "upb": np.ascontiguousarray(up_bias.reshape(HB, 128).T, f32),
            "Wcon": W_con,
            "bcon": b_con.reshape(1, -1),
            "downb": down_bias.reshape(1, -1),
            "wdU": _pack_sv(wdU_), "woU": _pack_sv(woU_), "woUp": _pack_sv(woUp_),
            "wdD": _pack_sv(wdD_), "woD": _pack_sv(woD_), "woDp": _pack_sv(woDp_),
            "identb": identb, "identf": identf, "perms": perms, "ones1": ones1,
            "selmat": selmat,
            "nonce": nonce,
            "role": np.array([[h, 1 if p == 0 else 0, 1 if p == 1 else 0,
                               0, 0, 0, 0, 0]], np.int32),
        }
        in_maps.append(m)

    res = run_bass_kernel_spmd(nc, in_maps, core_ids=list(range(8)))
    out = np.empty((2, SEQ, DM), f32)
    for c in range(8):
        b = c // 4
        p = (c // 2) % 2
        h = c % 2
        qlo = p * 8192 + h * 4096
        out[b, qlo:qlo + QTOK] = res.results[c]["mout"]
    return out


# version 20
# speedup vs baseline: 1.5723x; 1.5723x over previous
import numpy as np
import ml_dtypes
import concourse.bass as bass
import concourse.mybir as mybir
from concourse.bass_utils import run_bass_kernel_spmd
from contextlib import ExitStack

F32 = mybir.dt.float32
BF16 = mybir.dt.bfloat16
I32 = mybir.dt.int32
AF = mybir.ActivationFunctionType
ALU = mybir.AluOpType
AX = mybir.AxisListType

MAGIC = 1414213
CH = 512
NCHUNK = 32
G = 8
NKC = 8
HB = 32
NST = 16
DM = 1024
DHID = 4096
SEQ = 16384
QTOK = 4096
BW = 1032
EPS_ATTN = 1e-6
EPS_OUT = 1e-5
FT = 256
ENGINES = ("sync", "scalar", "vector", "tensor", "gpsimd")
DEBUG = False
DEBUG2 = False
TESTDELAY = False
DEBUG3 = False


class Sched:
    def __init__(self):
        self.ops = {e: [] for e in ENGINES}
        self.count = {}

    def emit(self, e, fn):
        self.ops[e].append(fn)

    def inc(self, s, amount):
        v = self.count.get(s, 0) + amount
        self.count[s] = v
        return v

    def cur(self, s):
        return self.count.get(s, 0)


def _stage_vectors(weights):
    S_, dim = weights.shape[0], weights.shape[1] * 2
    wd = np.zeros((S_, dim), np.float32)
    wo = np.zeros((S_, dim), np.float32)
    for st in range(S_):
        s = 2 ** st
        nb = dim // (2 * s)
        w = weights[st].reshape(nb, s, 2, 2)
        i = np.arange(dim)
        blk = i // (2 * s)
        sub = (i % (2 * s)) % s
        hi = (i % (2 * s)) // s
        wd[st] = np.where(hi == 0, w[blk, sub, 0, 0], w[blk, sub, 1, 1])
        wo[st] = np.where(hi == 0, w[blk, sub, 0, 1], w[blk, sub, 1, 0])
    return wd, wo


def _perm(v):
    S_, dim = v.shape
    out = np.empty_like(v)
    for s in range(S_):
        out[s] = v[s][np.arange(dim) ^ (2 ** s)]
    return out


def _pack_sv(v):
    return np.ascontiguousarray(
        v.reshape(12, 32, 128).transpose(2, 0, 1).reshape(128, 12 * 32))


def build_program():
    nc = bass.Bass("TRN2", target_bir_lowering=False)

    def din(name, shape, dt=F32):
        return nc.dram_tensor(name, shape, dt, kind="ExternalInput")

    x_full = din("x_full", [SEQ, DM])
    x_own = din("x_own", [QTOK, DM])
    Wi = din("Wi", [DM, 2048])
    convw = din("convw", [128, G * 4])
    convb = din("convb", [128, G])
    Wxp = din("Wxp", [1024, 96])
    Wdt = din("Wdt", [64, 1024])
    bdt = din("bdt", [128, G])
    Alog = din("Alog", [128, G * NST])
    Dp = din("Dp", [128, G])
    Wout = din("Wout", [2048, DM])
    gamma = din("gamma", [1, DM])
    beta = din("beta", [1, DM])
    Wexp = din("Wexp", [DM, DHID])
    bexp = din("bexp", [1, DHID])
    upb = din("upb", [128, HB])
    Wcon = din("Wcon", [DHID, DM])
    bcon = din("bcon", [1, DM])
    downb = din("downb", [1, DM])
    wdU = din("wdU", [128, 12 * HB]); woU = din("woU", [128, 12 * HB]); woUp = din("woUp", [128, 12 * HB])
    wdD = din("wdD", [128, 12 * HB]); woD = din("woD", [128, 12 * HB]); woDp = din("woDp", [128, 12 * HB])
    identb_in = din("identb", [128, 128], BF16)
    identf_in = din("identf", [128, 128])
    perms_in = din("perms", [128, 7 * 128])
    ones1_in = din("ones1", [1, 128])
    selmat_in = din("selmat", [32, 32 * 128], BF16)
    role_in = din("role", [1, 8], I32)
    nonce_in = din("nonce", [1, 16], I32)

    mout = nc.dram_tensor("mout", [QTOK, DM], F32, kind="ExternalOutput")
    if DEBUG or DEBUG2 or DEBUG3:
        d_wexp = nc.dram_tensor("d_wexp", [128, NKC * DHID], BF16, kind="ExternalOutput")
        d_wcon = nc.dram_tensor("d_wcon", [128, HB * DM], BF16, kind="ExternalOutput")
        d_x1 = nc.dram_tensor("d_x1", [QTOK, DM], F32, kind="ExternalOutput")
        d_gated = nc.dram_tensor("d_gated", [2048, 8192], BF16, kind="ExternalOutput")
        d_xdbl = nc.dram_tensor("d_xdbl", [2 * NCHUNK * 96, CH], F32, kind="ExternalOutput")
        d_m = nc.dram_tensor("d_m", [QTOK, DM], F32, kind="ExternalOutput")
        d_gld = nc.dram_tensor("d_gld", [128, 16 * 128], BF16, kind="ExternalOutput")
        d_stat = nc.dram_tensor("d_stat", [32 * 128, 8], F32, kind="ExternalOutput")
        d_beff = nc.dram_tensor("d_beff", [128, HB], F32, kind="ExternalOutput")
        d_lnu2 = nc.dram_tensor("d_lnu2", [256, DM], F32, kind="ExternalOutput")
        d_gelt = nc.dram_tensor("d_gelt", [128, HB * FT], BF16, kind="ExternalOutput")
        d_x1fm = nc.dram_tensor("d_x1fm", [128, NKC * FT], BF16, kind="ExternalOutput")
        d_wexpS = nc.dram_tensor("d_wexpS", [128, NKC * DHID], BF16, kind="ExternalOutput")
        d_wexpS2 = nc.dram_tensor("d_wexpS2", [128, NKC * DHID], BF16, kind="ExternalOutput")
        d_wexp_pre = nc.dram_tensor("d_wexp_pre", [128, NKC * DHID], BF16, kind="ExternalOutput")
        d_hp = nc.dram_tensor("d_hp", [128, FT], F32, kind="ExternalOutput")
        d_yp = nc.dram_tensor("d_yp", [128, DM], F32, kind="ExternalOutput")
        d_x1t = nc.dram_tensor("d_x1t", [128, 2 * DM], F32, kind="ExternalOutput")
        d_lnu = nc.dram_tensor("d_lnu", [QTOK, DM], F32, kind="ExternalOutput")

    sh_xdbl = nc.dram_tensor("sh_xdbl", [2, NCHUNK, 96, CH], F32, addr_space="Shared")
    sh_gated = nc.dram_tensor("sh_gated", [2048, 8192], BF16, addr_space="Shared")
    sh_flags = nc.dram_tensor("sh_flags", [2, 64, 16], I32, addr_space="Shared")
    wexp_d = nc.dram_tensor("wexp_d", [128, NKC * DHID], BF16)
    wcon_d = nc.dram_tensor("wcon_d", [128, HB * DM], BF16)
    x1_d = nc.dram_tensor("x1_d", [QTOK, DM], F32)
    gbc_d = nc.dram_tensor("gbc_d", [128, DM], F32)
    bbc_d = nc.dram_tensor("bbc_d", [128, DM], F32)
    b2bc_d = nc.dram_tensor("b2bc_d", [128, DM], F32)

    S = Sched()

    with ExitStack() as ctx:
        def sb(name, shape, dt=F32, c=None):
            return (c or ctx).enter_context(nc.sbuf_tensor(name, shape, dt))

        def mkpsum(name, shape, dt=F32):
            return ctx.enter_context(nc.psum_tensor(name, shape, dt))

        Atile = sb("Atile", [128, G * NST])
        convwS = sb("convwS", [128, G * 4])
        convbS = sb("convbS", [128, G])
        bdtS = sb("bdtS", [128, G])
        DpS = sb("DpS", [128, G])
        identb = sb("identb_s", [128, 128], BF16)
        ones1 = sb("ones1_s", [1, 128])
        roleS = sb("roleS", [1, 8], I32)
        magicS = sb("magicS", [1, 16], I32)
        beff = sb("beff", [128, HB])
        epsA = sb("epsA", [128, 1])
        epsO = sb("epsO", [128, 1])
        carry = sb("carry", [128, G * NST])
        flg = sb("flg", [1, 16], I32)

        pA = mkpsum("pA", [128, 1024])
        pB = mkpsum("pB", [128, 1024])
        pC = mkpsum("pC", [128, 512])
        pD = mkpsum("pD", [128, 512])
        pE = mkpsum("pE", [128, 1024], BF16)
        pF = mkpsum("pF", [128, 1024], BF16)

        sems = {}
        for s_ in ("LD", "GD", "PL", "PE", "DV", "AC", "PO", "ST"):
            sems[s_] = ctx.enter_context(nc.semaphore(s_))

        regs = {}
        for rn, eng_ in (("g_h", nc.gpsimd), ("g_cnt", nc.gpsimd), ("g_flag", nc.gpsimd),
                         ("g_go", nc.gpsimd), ("g_ro", nc.gpsimd), ("g_nonce", nc.gpsimd),
                         ("s_h", nc.sync), ("v_ro", nc.vector), ("t_ro", nc.tensor)):
            regs[rn] = ctx.enter_context(eng_.register(rn))

        def E(e, fn):
            S.emit(e, fn)

        def wait(e, sname, val):
            if val > 0:
                E(e, lambda eng, s=sems[sname], v=val: eng.wait_ge(s, v))

        def dma(e, out_ap, in_ap, sname):
            v = S.inc(sname, 16)
            E(e, lambda eng, o=out_ap, i=in_ap, s=sems[sname]:
              eng.dma_start(out=o, in_=i).then_inc(s, 16))
            return v

        def op(e, fn, sname):
            v = S.inc(sname, 1)
            E(e, lambda eng, f=fn, s=sems[sname]: f(eng).then_inc(s, 1))
            return v

        def opn(e, fn):
            E(e, fn)

        def opd(e, fn, sname="DV"):
            v = op(e, fn, sname)
            wait(e, sname, v)
            return v

        _guards = {e: [] for e in ENGINES}

        def g_if(e, regname, val):
            def fn(eng, rn=regname, v=val):
                gd = eng.If_eq(regs[rn], v)
                gd.__enter__()
                _guards[e].append(gd)
            E(e, fn)

        def g_else(e):
            def fn(eng, ee=e):
                _guards[ee].pop().__exit__(None, None, None)
                gd = eng.Else()
                gd.__enter__()
                _guards[ee].append(gd)
            E(e, fn)

        def g_end(e):
            def fn(eng, ee=e):
                _guards[ee].pop().__exit__(None, None, None)
            E(e, fn)

        def balance(e, sname, amount):
            E(e, lambda eng, s=sems[sname], a=amount: eng.sem_inc(s, a))

        def cdma_h(e, out0, in0, out1, in1, sname):
            hreg = "g_h" if e == "gpsimd" else "s_h"
            v = S.inc(sname, 16)
            g_if(e, hreg, 0)
            E(e, lambda eng, o=out0, i=in0, s=sems[sname]:
              eng.dma_start(out=o, in_=i).then_inc(s, 16))
            g_else(e)
            E(e, lambda eng, o=out1, i=in1, s=sems[sname]:
              eng.dma_start(out=o, in_=i).then_inc(s, 16))
            g_end(e)
            return v

        bankfree = {}

        def pe_acquire(*banks):
            for bank in banks:
                ev = bankfree.get(bank)
                if ev:
                    wait("tensor", ev[0], ev[1])

        def bank_freed(bank, sname, val):
            bankfree[bank] = (sname, val)

        def poll(cell0, cell1):
            def fn(eng):
                eng.reg_mov(regs["g_go"], 1)
                wg = eng.While(regs["g_go"])
                wg.__enter__()
                with eng.If_eq(regs["g_h"], 0):
                    eng.dma_start(out=flg[:, :], in_=cell1).then_inc(sems["PL"], 16)
                with eng.Else():
                    eng.dma_start(out=flg[:, :], in_=cell0).then_inc(sems["PL"], 16)
                eng.reg_add(regs["g_cnt"], regs["g_cnt"], 16)
                eng.wait_ge(sems["PL"], regs["g_cnt"])
                eng.reg_load(regs["g_flag"], flg[:1, :1])
                eng.reg_sub(regs["g_go"], regs["g_flag"], regs["g_nonce"])
                wg.__exit__(None, None, None)
            E("gpsimd", fn)

        mctx = ExitStack()
        wiS = sb("wiS", [128, 16 * NKC * 128], BF16, c=mctx)
        wxpS = sb("wxpS", [128, G * 96], BF16, c=mctx)
        wdtS = sb("wdtS", [64, G * 128], BF16, c=mctx)

        pctx = ExitStack()
        identf = sb("identf_s", [128, 128], c=pctx)
        permsS = sb("permsS", [128, 7 * 128], c=pctx)
        grow = sb("grow", [1, DM], c=pctx); brow = sb("brow", [1, DM], c=pctx)
        b2row = sb("b2row", [1, DM], c=pctx); dbrow = sb("dbrow", [1, DM], c=pctx)
        wdUS = sb("wdUS", [128, 12 * HB], c=pctx); woUS = sb("woUS", [128, 12 * HB], c=pctx)
        woUPS = sb("woUPS", [128, 12 * HB], c=pctx)
        wdDS = sb("wdDS", [128, 12 * HB], c=pctx); woDS = sb("woDS", [128, 12 * HB], c=pctx)
        woDPS = sb("woDPS", [128, 12 * HB], c=pctx)
        cbuf = sb("cbuf", [128, HB * BW], BF16, c=pctx)
        smat = sb("smat", [128, 128], c=pctx)
        smat2 = sb("smat2", [128, 128], c=pctx)
        smatb = sb("smatb", [128, 128], BF16, c=pctx)
        smatb2 = sb("smatb2", [128, 128], BF16, c=pctx)
        wtmp = sb("wtmp", [128, DHID], BF16, c=pctx)

        for dst, src in ((roleS, role_in), (magicS, nonce_in), (identf, identf_in), (permsS, perms_in),
                         (ones1, ones1_in), (Atile, Alog), (convwS, convw),
                         (convbS, convb), (bdtS, bdt), (DpS, Dp), (identb, identb_in),
                         (grow, gamma), (brow, beta), (b2row, bcon), (dbrow, downb),
                         (wdUS, wdU), (woUS, woU), (woUPS, woUp),
                         (wdDS, wdD), (woDS, woD), (woDPS, woDp), (beff, upb)):
            dma("sync", dst[:, :], src.ap(), "LD")
        ld0 = S.cur("LD")
        for kc in range(NKC):
            dma("gpsimd",
                bass.AP(wiS, kc * 128,
                        [[16384, 128], [1024, 16], [1, 128]]),
                bass.AP(Wi, kc * 128 * 2048,
                        [[2048, 128], [128, 16], [1, 128]]), "GD")
        dma("gpsimd", wxpS[:, :],
            bass.AP(Wxp, 0, [[96, 128], [128 * 96, G], [1, 96]]), "GD")
        dma("gpsimd", wdtS[:, :], Wdt.ap(), "GD")
        gd_w = S.cur("GD")

        wait("gpsimd", "LD", ld0)
        opn("gpsimd", lambda eng: eng.reg_load(regs["g_h"], roleS[:1, 0:1]))
        opn("gpsimd", lambda eng: eng.reg_load(regs["g_nonce"], magicS[:1, 0:1]))
        opn("gpsimd", lambda eng: eng.reg_mov(regs["g_cnt"], 0))
        wait("sync", "LD", ld0)
        opn("sync", lambda eng: eng.reg_load(regs["s_h"], roleS[:1, 0:1]))
        wait("vector", "LD", ld0)
        wait("scalar", "LD", ld0)
        wait("tensor", "LD", ld0)

        aa = op("scalar", lambda eng: eng.activation(Atile[:, :], Atile[:, :], AF.Exp), "AC")
        wait("vector", "AC", aa)
        op("vector", lambda eng: eng.tensor_scalar_mul(Atile[:, :], Atile[:, :], -1.0), "DV")
        op("vector", lambda eng: eng.tensor_add(b2row[:, :], b2row[:, :], dbrow[:, :]), "DV")
        op("vector", lambda eng: eng.memset(carry[:, :], 0.0), "DV")
        op("vector", lambda eng: eng.memset(epsA[:, :], EPS_ATTN), "DV")
        op("vector", lambda eng: eng.memset(epsO[:, :], EPS_OUT), "DV")
        dv_init = S.cur("DV")

        wait("tensor", "DV", dv_init)
        bctmp = sb("bctmp", [128, DM], c=pctx)
        for row, dstd in ((grow, gbc_d), (brow, bbc_d), (b2row, b2bc_d)):
            for half in range(2):
                bank = "pC" if half == 0 else "pD"
                Pb = pC if half == 0 else pD
                pe_acquire(bank)
                pv = op("tensor", lambda eng, r=row, hf=half, P=Pb: eng.matmul(
                    P[:, :], ones1[:, :], r[:1, hf * 512:(hf + 1) * 512],
                    start=True, stop=True), "PE")
                wait("vector", "PE", pv)
                dv_ = op("vector", lambda eng, hf=half, P=Pb: eng.tensor_copy(
                    bctmp[:, hf * 512:(hf + 1) * 512], P[:, :]), "DV")
                bank_freed(bank, "DV", dv_)
            wait("gpsimd", "DV", S.cur("DV"))
            dma("gpsimd", dstd.ap(), bctmp[:, :], "GD")
            wait("vector", "GD", S.cur("GD"))

        def build_smat(col, vec_d, vec_o, s_i):
            op("vector", lambda eng, c=col, v=vec_d: eng.tensor_scalar_mul(
                smat[:, :], identf[:, :], v[:, c:c + 1]), "DV")
            if s_i <= 6:
                op("vector", lambda eng, c=col, v=vec_o, si=s_i: eng.tensor_scalar_mul(
                    smat2[:, :], permsS[:, si * 128:(si + 1) * 128], v[:, c:c + 1]), "DV")
                op("vector", lambda eng: eng.tensor_add(smat[:, :], smat[:, :], smat2[:, :]), "DV")
                return op("vector", lambda eng: eng.tensor_copy(smatb[:, :], smat[:, :]), "DV")
            else:
                op("vector", lambda eng, c=col, v=vec_o: eng.tensor_scalar_mul(
                    smat2[:, :], identf[:, :], v[:, c:c + 1]), "DV")
                op("vector", lambda eng: eng.tensor_copy(smatb[:, :], smat[:, :]), "DV")
                return op("vector", lambda eng: eng.tensor_copy(smatb2[:, :], smat2[:, :]), "DV")

        def mm_block(dstP, lhsT_t, src_off, start, stop, has_bias):
            for hf in range(2):
                op("tensor", lambda eng, P=dstP, L=lhsT_t, o=src_off + hf * 512,
                   O=hf * 512, st=start, sp=stop: eng.matmul(
                       P[:, O:O + 512], L[:, :], cbuf[:, o:o + 512], start=st, stop=sp), "PE")
            if has_bias:
                op("tensor", lambda eng, L=lhsT_t, o=src_off + DM, st=start, sp=stop:
                   eng.matmul(pC[:, 0:1], L[:, :], cbuf[:, o:o + 1], start=st, stop=sp), "PE")
            return S.cur("PE")

        def compose(wd_t, wo_t, wop_t, reverse, has_bias):
            order = list(range(11, -1, -1)) if reverse else list(range(12))
            for s_i in order:
                if s_i <= 6:
                    vo = wo_t if reverse else wop_t
                    for blk in range(HB):
                        col = s_i * HB + blk
                        v1 = build_smat(col, wd_t, vo, s_i)
                        wait("tensor", "DV", v1)
                        pe_acquire("pA")
                        if has_bias:
                            pe_acquire("pC")
                        pv = mm_block(pA, smatb, blk * BW, True, True, has_bias)
                        wait("vector", "PE", pv)
                        dva = op("vector", lambda eng, b=blk: eng.tensor_copy(
                            cbuf[:, b * BW: b * BW + DM], pA[:, :]), "DV")
                        if has_bias:
                            dvc = op("vector", lambda eng, b=blk: eng.tensor_copy(
                                cbuf[:, b * BW + DM: b * BW + DM + 1], pC[:, 0:1]), "DV")
                            bank_freed("pC", "DV", dvc)
                        bank_freed("pA", "DV", dva)
                        wait("vector", "PE", pv)
                else:
                    vo = wop_t if reverse else wo_t
                    for blk in range(HB):
                        partner = blk ^ (2 ** (s_i - 7))
                        if partner < blk:
                            continue
                        col = s_i * HB + blk
                        colp = s_i * HB + partner
                        v1 = build_smat(col, wd_t, vo, s_i)
                        wait("tensor", "DV", v1)
                        pe_acquire("pA")
                        if has_bias:
                            pe_acquire("pC")
                        mm_block(pA, smatb, blk * BW, True, False, has_bias)
                        mm_block(pA, smatb2, partner * BW, False, True, has_bias)
                        v2 = build_smat(colp, wd_t, vo, s_i)
                        wait("tensor", "DV", v2)
                        pe_acquire("pB")
                        if has_bias:
                            pe_acquire("pD")
                        for hf in range(2):
                            op("tensor", lambda eng, o=partner * BW + hf * 512, O=hf * 512:
                               eng.matmul(pB[:, O:O + 512], smatb[:, :], cbuf[:, o:o + 512],
                                          start=True, stop=False), "PE")
                        for hf in range(2):
                            op("tensor", lambda eng, o=blk * BW + hf * 512, O=hf * 512:
                               eng.matmul(pB[:, O:O + 512], smatb2[:, :], cbuf[:, o:o + 512],
                                          start=False, stop=True), "PE")
                        if has_bias:
                            op("tensor", lambda eng, o=partner * BW + DM: eng.matmul(
                                pD[:, 0:1], smatb[:, :], cbuf[:, o:o + 1],
                                start=True, stop=False), "PE")
                            op("tensor", lambda eng, o=blk * BW + DM: eng.matmul(
                                pD[:, 0:1], smatb2[:, :], cbuf[:, o:o + 1],
                                start=False, stop=True), "PE")
                        pv2 = S.cur("PE")
                        wait("vector", "PE", pv2)
                        dva = op("vector", lambda eng, b=blk: eng.tensor_copy(
                            cbuf[:, b * BW: b * BW + DM], pA[:, :]), "DV")
                        dvb = op("vector", lambda eng, b=partner: eng.tensor_copy(
                            cbuf[:, b * BW: b * BW + DM], pB[:, :]), "DV")
                        bank_freed("pA", "DV", dva)
                        bank_freed("pB", "DV", dvb)
                        if has_bias:
                            dvc = op("vector", lambda eng, b=blk: eng.tensor_copy(
                                cbuf[:, b * BW + DM: b * BW + DM + 1], pC[:, 0:1]), "DV")
                            dvd = op("vector", lambda eng, b=partner: eng.tensor_copy(
                                cbuf[:, b * BW + DM: b * BW + DM + 1], pD[:, 0:1]), "DV")
                            bank_freed("pC", "DV", dvc)
                            bank_freed("pD", "DV", dvd)
                        wait("vector", "PE", pv2)

        for kc in range(NKC):
            if kc > 0:
                wait("gpsimd", "PE", S.cur("PE"))
            gv = dma("gpsimd", wtmp[:, :], Wexp.ap()[kc * 128:(kc + 1) * 128, :], "GD")
            wait("tensor", "GD", gv)
            for hb in range(HB):
                par = hb % 2
                bank = "pE" if par == 0 else "pF"
                P_ = pE if par == 0 else pF
                pe_acquire(bank)
                pv = op("tensor", lambda eng, H=hb, P=P_: eng.transpose(
                    P[:, 0:128], wtmp[:, H * 128:(H + 1) * 128], identb[:, :]), "PE")
                wait("vector", "PE", pv)
                dv_ = op("vector", lambda eng, H=hb, K=kc, P=P_: eng.tensor_copy(
                    cbuf[:, H * BW + K * 128: H * BW + (K + 1) * 128], P[:, 0:128]), "DV")
                bank_freed(bank, "DV", dv_)
        for hb in range(HB):
            dma("gpsimd", cbuf[:, hb * BW + DM: hb * BW + DM + 1],
                bexp.ap()[0:1, hb * 128:(hb + 1) * 128].rearrange("a b -> b a"), "GD")
        gv = S.cur("GD")
        wait("vector", "GD", gv)
        dvp = op("vector", lambda eng: eng.memset(
            cbuf.ap().rearrange("p (hb w) -> p hb w", hb=HB, w=BW)[:, :, DM + 1:], 0.0), "DV")
        wait("tensor", "DV", dvp)
        compose(wdUS, woUS, woUPS, False, True)
        op("vector", lambda eng: eng.tensor_add(
            beff[:, :], beff[:, :],
            cbuf.ap().rearrange("p (hb w) -> p hb w", hb=HB, w=BW)[:, :, DM]), "DV")
        gd_stage_prev = 0
        for kc in range(NKC):
            if gd_stage_prev:
                wait("vector", "GD", gd_stage_prev)
            for hb in range(HB):
                par = hb % 2
                bank = "pE" if par == 0 else "pF"
                P_ = pE if par == 0 else pF
                pe_acquire(bank)
                pv = op("tensor", lambda eng, H=hb, K=kc, P=P_: eng.transpose(
                    P[:, 0:128], cbuf[:, H * BW + K * 128: H * BW + (K + 1) * 128],
                    identb[:, :]), "PE")
                wait("vector", "PE", pv)
                if kc > 0:
                    pass
                dv_ = op("vector", lambda eng, H=hb, P=P_: eng.tensor_copy(
                    wtmp[:, H * 128:(H + 1) * 128], P[:, 0:128]), "DV")
                bank_freed(bank, "DV", dv_)
            dvw = S.cur("DV")
            wait("gpsimd", "DV", dvw)
            gd_stage_prev = dma("gpsimd", wexp_d.ap()[:, kc * DHID:(kc + 1) * DHID], wtmp[:, :], "GD")
        gd_up = S.cur("GD")

        wait("gpsimd", "PE", S.cur("PE"))
        wait("gpsimd", "DV", S.cur("DV"))
        for hc in range(HB):
            dma("gpsimd", cbuf[:, hc * BW: hc * BW + DM],
                Wcon.ap()[hc * 128:(hc + 1) * 128, :], "GD")
        gv = S.cur("GD")
        wait("vector", "GD", gv)
        dvp2 = op("vector", lambda eng: eng.memset(
            cbuf.ap().rearrange("p (hb w) -> p hb w", hb=HB, w=BW)[:, :, DM:], 0.0), "DV")
        wait("tensor", "DV", dvp2)
        compose(wdDS, woDS, woDPS, True, False)
        wait("gpsimd", "DV", S.cur("DV"))
        for hc in range(HB):
            dma("gpsimd", wcon_d.ap()[:, hc * DM:(hc + 1) * DM],
                cbuf[:, hc * BW: hc * BW + DM], "GD")
        gd_dn = S.cur("GD")
        wait("gpsimd", "GD", gd_dn)
        wait("gpsimd", "PE", S.cur("PE"))
        wait("gpsimd", "DV", S.cur("DV"))
        if DEBUG:
            dma("gpsimd", d_wexp_pre.ap(), wexp_d.ap(), "GD")
            wait("gpsimd", "GD", S.cur("GD"))
        po_prep = op("gpsimd", lambda eng: eng.engine_nop(), "PO")

        pctx.close()

        fctx = ExitStack()
        xrb = sb("xrb", [128, DM], BF16, c=fctx)
        xfm = sb("xfm", [128, NKC * CH], BF16, c=fctx)
        xi = sb("xi", [128, G * (CH + 3)], c=fctx)
        xcv = sb("xcv", [128, G * CH], c=fctx)
        xcbP = sb("xcbP", [128, 2 * G * CH], BF16, c=fctx)
        gzbP = sb("gzbP", [128, 2 * G * CH], BF16, c=fctx)
        dtb = sb("dtb", [128, G * CH], c=fctx)
        dtmp = sb("dtmp", [128, CH], c=fctx)
        dtg = sb("dtg", [128, CH], c=fctx)
        vbf = sb("vbf", [128, G * CH], BF16, c=fctx)
        xdrO = sb("xdrO", [64, 2 * CH], c=fctx)
        xbcO = sb("xbcO", [32, 2 * CH], c=fctx)
        xdr = sb("xdr", [64, CH], c=fctx)
        xbcB = sb("xbcB", [32, CH], BF16, c=fctx)
        xdrQ = sb("xdrQ", [64, CH], c=fctx)
        xbcQ = sb("xbcQ", [32, CH], c=fctx)
        selS = sb("selS", [32, 32 * 128], BF16, c=fctx)
        dtrb = sb("dtrb", [64, CH], BF16, c=fctx)
        Bbc = sb("Bbc", [128, 8 * CH], BF16, c=fctx)
        Cbc = sb("Cbc", [128, 8 * CH], BF16, c=fctx)
        hG = sb("hG", [128, 2 * CH], BF16, c=fctx)
        yall = sb("yall", [128, G * CH], BF16, c=fctx)
        qt0 = sb("qt0", [128, CH], BF16, c=fctx)
        qt1 = sb("qt1", [128, CH], BF16, c=fctx)
        dAb = sb("dAb", [128, 4 * CH], BF16, c=fctx)
        dBb = sb("dBb", [128, 4 * CH], BF16, c=fctx)
        gt = sb("gt", [128, 2 * CH], BF16, c=fctx)

        wait("vector", "PO", po_prep)
        wait("scalar", "PO", po_prep)
        wait("tensor", "PO", po_prep)
        wait("sync", "PO", po_prep)
        wait("gpsimd", "PO", po_prep)
        dv_xi0 = op("vector", lambda eng: eng.memset(xi[:, :], 0.0), "DV")
        wait("scalar", "DV", dv_xi0)
        ls_sel = dma("sync", selS[:, :], selmat_in.ap(), "LD")
        wait("tensor", "LD", ls_sel)

        trk = {}
        ldx = {}
        win_pe = {}
        evac_st = {}
        tail_dv = {}
        conv_dv = {}
        conv_g_dv = {}
        silu_ac = {}
        silu_g_ac = {}
        dtb_g_dv = {}
        vbf_g_dv = {}
        xfm_ac = {}
        exp_last = {}
        pool_last = {}
        selpe_last = {}
        xdradd_dv = {}
        bbc_dv = {}
        dbu_last_slot = {}
        slot_dv = [0, 0, 0, 0]
        slot_ctr = [0]
        WBANK = [("pAa", 0, pA), ("pAb", 512, pA), ("pC", 0, pC), ("pD", 0, pD)]

        def emit_roles(colix):
            opn("vector", lambda eng, C=colix: eng.reg_load(regs["v_ro"], roleS[:1, C:C + 1]))
            opn("tensor", lambda eng, C=colix: eng.reg_load(regs["t_ro"], roleS[:1, C:C + 1]))
            opn("gpsimd", lambda eng, C=colix: eng.reg_load(regs["g_ro"], roleS[:1, C:C + 1]))

        def xload(k, sub):
            prev = trk.get((k, sub - 1) if sub > 0 else (k - 1, 3), 0)
            wait("gpsimd", "PE", prev)
            ldx[(k, sub)] = dma(
                "gpsimd", xrb[:, :],
                x_full.ap()[k * CH + sub * 128: k * CH + (sub + 1) * 128, :], "GD")

        def emit_transposes(k):
            if k > 0:
                wait("scalar", "PE", win_pe[(k - 1, 15)])
            for sub in range(4):
                wait("tensor", "GD", ldx[(k, sub)])
                for kc in range(NKC):
                    pb = "pE" if kc % 2 == 0 else "pF"
                    P_ = pE if kc % 2 == 0 else pF
                    pe_acquire(pb)
                    pv = op("tensor", lambda eng, K=kc, P=P_: eng.transpose(
                        P[:, 0:128], xrb[:, K * 128:(K + 1) * 128],
                        identb[:, :]), "PE")
                    wait("scalar", "PE", pv)
                    av = op("scalar", lambda eng, K=kc, Sb=sub, P=P_: eng.activation(
                        xfm[:, K * CH + Sb * 128: K * CH + Sb * 128 + 128],
                        P[:, 0:128], AF.Copy), "AC")
                    bank_freed(pb, "AC", av)
                trk[(k, sub)] = S.cur("PE")
                if sub < 3:
                    xload(k, sub + 1)
            xfm_ac[k] = S.cur("AC")

        def emit_win(k):
            wait("tensor", "AC", xfm_ac[k])
            for fb in range(16):
                bname, off, P_ = WBANK[fb % 4]
                if fb >= 4:
                    wait("tensor", "ST", evac_st[(k, fb - 4)])
                else:
                    pe_acquire(bname)
                for kc in range(NKC):
                    op("tensor", lambda eng, F=fb, K=kc, O=off, P=P_: eng.matmul(
                        P[:, O:O + CH], wiS[:, (F * NKC + K) * 128:(F * NKC + K + 1) * 128],
                        xfm[:, K * CH:(K + 1) * CH],
                        start=(K == 0), stop=(K == NKC - 1)), "PE")
                win_pe[(k, fb)] = S.cur("PE")

        def make_evacs(k):
            parA = k % 2
            tasks = []
            for fb in range(16):
                stv = S.inc("ST", 1)
                evac_st[(k, fb)] = stv

                def t(fb=fb, k=k, parA=parA, stv=stv):
                    bname, off, P_ = WBANK[fb % 4]
                    if fb == 0 and k > 0:
                        wait("scalar", "DV", tail_dv.get(k, 0))
                    wait("scalar", "PE", win_pe[(k, fb)])
                    if fb < 8:
                        E("scalar", lambda eng, F=fb, O=off, P=P_, s=sems["ST"]: eng.activation(
                            xi[:, F * (CH + 3) + 3: F * (CH + 3) + 3 + CH],
                            P[:, O:O + CH], AF.Copy).then_inc(s, 1))
                    else:
                        E("scalar", lambda eng, F=fb, O=off, P=P_, PA=parA, s=sems["ST"]:
                          eng.activation(
                              gzbP[:, PA * G * CH + (F - 8) * CH: PA * G * CH + (F - 7) * CH],
                              P[:, O:O + CH], AF.Silu).then_inc(s, 1))
                    bank_freed(bname, "ST", stv)
                tasks.append(t)
            return tasks

        def make_conv(k):
            tasks = []
            for g in range(G):
                def t(g=g, k=k):
                    base = g * (CH + 3)
                    wait("vector", "ST", evac_st[(k, g)])
                    if g == 0 and k > 0:
                        wait("vector", "AC", silu_ac.get(k - 1, 0))
                    op("vector", lambda eng, b=base, gg=g: eng.tensor_scalar(
                        xcv[:, gg * CH:(gg + 1) * CH], xi[:, b: b + CH],
                        convwS[:, gg * 4: gg * 4 + 1], convbS[:, gg: gg + 1],
                        ALU.mult, ALU.add), "DV")
                    for kk in range(1, 4):
                        op("vector", lambda eng, b=base, gg=g, K=kk: eng.scalar_tensor_tensor(
                            xcv[:, gg * CH:(gg + 1) * CH], xi[:, b + K: b + K + CH],
                            convwS[:, gg * 4 + K: gg * 4 + K + 1], xcv[:, gg * CH:(gg + 1) * CH],
                            ALU.mult, ALU.add), "DV")
                    conv_g_dv[(k, g)] = S.cur("DV")
                    if g == G - 1:
                        conv_dv[k] = S.cur("DV")
                tasks.append(t)
            return tasks

        def make_silus(k):
            parA = k % 2
            tasks = []
            for g in range(G):
                def t(g=g, k=k, parA=parA):
                    wait("scalar", "DV", conv_g_dv[(k, g)])
                    av = op("scalar", lambda eng, gg=g, PA=parA: eng.activation(
                        xcbP[:, PA * G * CH + gg * CH: PA * G * CH + (gg + 1) * CH],
                        xcv[:, gg * CH:(gg + 1) * CH], AF.Silu), "AC")
                    silu_g_ac[(k, g)] = av
                    if g == G - 1:
                        silu_ac[k] = av
                tasks.append(t)
            return tasks

        def emit_Bhead1(kB):
            poll(sh_flags.ap()[0, kB], sh_flags.ap()[1, kB])
            wait("gpsimd", "DV", xdradd_dv.get(kB - 1, 0))
            cdma_h("gpsimd", xdrQ[:, :], sh_xdbl.ap()[1, kB][0:64, :],
                   xdrQ[:, :], sh_xdbl.ap()[0, kB][0:64, :], "GD")
            gq = cdma_h("gpsimd", xbcQ[:, :], sh_xdbl.ap()[1, kB][64:96, :],
                        xbcQ[:, :], sh_xdbl.ap()[0, kB][64:96, :], "GD")
            parB = kB % 2
            wait("vector", "GD", gq)
            wait("vector", "PE", selpe_last.get(kB - 1, 0))
            op("vector", lambda eng, PB=parB: eng.tensor_add(
                xdr[:, :], xdrO[:, PB * CH:(PB + 1) * CH], xdrQ[:, :]), "DV")
            op("vector", lambda eng, PB=parB: eng.tensor_add(
                xbcB[:, :], xbcO[:, PB * CH:(PB + 1) * CH], xbcQ[:, :]), "DV")
            xdradd_dv[kB] = S.cur("DV")

        def emit_Bhead2(kB):
            parB = kB % 2
            dv_dtr = op("vector", lambda eng: eng.tensor_copy(dtrb[:, :], xdr[:, :]), "DV")
            wait("tensor", "DV", dv_dtr)
            wait("vector", "AC", exp_last.get(kB - 1, 0))
            for g in range(G):
                bank = "pC" if g % 2 == 0 else "pD"
                P_ = pC if g % 2 == 0 else pD
                pe_acquire(bank)
                pv = op("tensor", lambda eng, gg=g, P=P_: eng.matmul(
                    P[:, :], wdtS[:, gg * 128:(gg + 1) * 128], dtrb[:, :],
                    start=True, stop=True), "PE")
                wait("vector", "PE", pv)
                dvp = op("vector", lambda eng, P=P_, gg=g: eng.tensor_scalar_add(
                    dtg[:, :], P[:, :], bdtS[:, gg:gg + 1]), "DV")
                bank_freed(bank, "DV", dvp)
                wait("scalar", "DV", dvp)
                op("scalar", lambda eng: eng.activation(dtmp[:, :], dtg[:, :], AF.Abs), "AC")
                op("scalar", lambda eng: eng.activation(dtmp[:, :], dtmp[:, :], AF.Exp, scale=-1.0), "AC")
                acl = op("scalar", lambda eng: eng.activation(dtmp[:, :], dtmp[:, :], AF.Ln, bias=1.0), "AC")
                op("vector", lambda eng: eng.tensor_scalar_max(dtg[:, :], dtg[:, :], 0.0), "DV")
                wait("vector", "AC", acl)
                op("vector", lambda eng, gg=g: eng.tensor_add(
                    dtb[:, gg * CH:(gg + 1) * CH], dtg[:, :], dtmp[:, :]), "DV")
                dtb_g_dv[(kB, g)] = S.cur("DV")
            wait("vector", "PO", pool_last.get(kB - 1, 0))
            wait("vector", "AC", silu_ac[kB])
            for g in range(G):
                op("vector", lambda eng, gg=g, PB=parB: eng.tensor_mul(
                    vbf[:, gg * CH:(gg + 1) * CH], dtb[:, gg * CH:(gg + 1) * CH],
                    xcbP[:, PB * G * CH + gg * CH: PB * G * CH + (gg + 1) * CH]), "DV")
                vbf_g_dv[(kB, g)] = S.cur("DV")

        def emit_sel(kB, nq):
            for j in range(2):
                n = nq * 2 + j
                slot = (nq % 4) * 2 + j
                bank = "pBa" if j == 0 else "pBb"
                off = 0 if j == 0 else 512
                pe_acquire(bank)
                if nq == 0 and j == 0:
                    wait("tensor", "DV", xdradd_dv[kB])
                pv = op("tensor", lambda eng, N=n, O=off: eng.matmul(
                    pB[:, O:O + CH], selS[:, N * 128:(N + 1) * 128], xbcB[:, :],
                    start=True, stop=True), "PE")
                wait("vector", "PE", pv)
                gprev = dbu_last_slot.get(slot, 0)
                if gprev:
                    wait("vector", "PO", gprev)
                dv_ = op("vector", lambda eng, SL=slot, O=off: eng.tensor_copy(
                    Bbc[:, SL * CH:(SL + 1) * CH], pB[:, O:O + CH]), "DV")
                bank_freed(bank, "DV", dv_)
                bbc_dv[(kB, slot)] = dv_
            for j in range(2):
                n = nq * 2 + j
                slot = (nq % 4) * 2 + j
                bank = "pBa" if j == 0 else "pBb"
                off = 0 if j == 0 else 512
                pe_acquire(bank)
                pv = S.inc("PE", 1)
                g_if("tensor", "t_ro", 1)
                E("tensor", lambda eng, N=n, O=off, s=sems["PE"]: eng.matmul(
                    pB[:, O:O + CH], selS[:, (16 + N) * 128:(16 + N + 1) * 128], xbcB[:, :],
                    start=True, stop=True).then_inc(s, 1))
                g_else("tensor")
                balance("tensor", "PE", 1)
                g_end("tensor")
                dv_ = S.inc("DV", 1)
                g_if("vector", "v_ro", 1)
                E("vector", lambda eng, s=sems["PE"], v=pv: eng.wait_ge(s, v))
                E("vector", lambda eng, SL=slot, O=off, s=sems["DV"]: eng.tensor_copy(
                    Cbc[:, SL * CH:(SL + 1) * CH], pB[:, O:O + CH]).then_inc(s, 1))
                g_else("vector")
                balance("vector", "DV", 1)
                g_end("vector")
                bank_freed(bank, "DV", dv_)
            selpe_last[kB] = S.cur("PE")

        def emit_B2(kB, spr):
            pend = None

            def flush_pend():
                nonlocal pend
                if pend is None:
                    return
                g_, nq_, dvh = pend
                pend = None
                wait("gpsimd", "DV", dvh)
                poc = op("gpsimd", lambda eng, gg=g_, NQ=nq_: eng.tensor_copy(
                    carry[:, gg * NST + NQ * 2: gg * NST + NQ * 2 + 2],
                    hG.ap().rearrange("p (j t) -> p j t", j=2, t=CH)[:, :, CH - 1]), "PO")
                wait("vector", "PO", poc)

            for nq in range(8):
                if nq >= 4:
                    emit_sel(kB, nq)
                for g in range(G):
                    ms = {}
                    for j in range(2):
                        slot = (nq % 4) * 2 + j
                        m = slot_ctr[0] % 4
                        ms[j] = m
                        slot_ctr[0] += 1
                        if slot_dv[m]:
                            wait("gpsimd", "DV", slot_dv[m])
                        wait("gpsimd", "DV", bbc_dv[(kB, slot)])
                        pov = op("gpsimd", lambda eng, gg=g, SL=slot, M=m: eng.tensor_mul(
                            dBb[:, M * CH:(M + 1) * CH], vbf[:, gg * CH:(gg + 1) * CH],
                            Bbc[:, SL * CH:(SL + 1) * CH]), "PO")
                        ms[(j, "pov")] = pov
                        dbu_last_slot[slot] = pov
                    flush_pend()
                    for j in range(2):
                        n = nq * 2 + j
                        slot = (nq % 4) * 2 + j
                        m = ms[j]
                        if slot_dv[m]:
                            wait("scalar", "DV", slot_dv[m])
                        if nq == 0:
                            wait("scalar", "DV", dtb_g_dv[(kB, g)])
                        aca = op("scalar", lambda eng, gg=g, N=n, M=m: eng.activation(
                            dAb[:, M * CH:(M + 1) * CH], dtb[:, gg * CH:(gg + 1) * CH],
                            AF.Exp, scale=Atile[:, gg * NST + N: gg * NST + N + 1]), "AC")
                        wait("vector", "AC", aca)
                        wait("vector", "PO", ms[(j, "pov")])
                        dvs = op("vector", lambda eng, gg=g, N=n, J=j, M=m: eng.tensor_tensor_scan(
                            hG[:, J * CH:(J + 1) * CH], dAb[:, M * CH:(M + 1) * CH],
                            dBb[:, M * CH:(M + 1) * CH],
                            carry[:, gg * NST + N: gg * NST + N + 1],
                            ALU.mult, ALU.add), "DV")
                        slot_dv[m] = dvs
                        S.inc("DV", 1)
                        g_if("vector", "v_ro", 1)
                        E("vector", lambda eng, J=j, SL=slot, QT=(qt0 if j == 0 else qt1),
                          s=sems["DV"]: eng.tensor_mul(
                              QT[:, :], hG[:, J * CH:(J + 1) * CH],
                              Cbc[:, SL * CH:(SL + 1) * CH]).then_inc(s, 1))
                        g_else("vector")
                        balance("vector", "DV", 1)
                        g_end("vector")
                    S.inc("DV", 1)
                    g_if("vector", "v_ro", 1)
                    E("vector", lambda eng: eng.tensor_add(qt0[:, :], qt0[:, :], qt1[:, :]))
                    if nq == 0:
                        E("vector", lambda eng, gg=g, s=sems["DV"]: eng.tensor_copy(
                            yall[:, gg * CH:(gg + 1) * CH], qt0[:, :]).then_inc(s, 1))
                    else:
                        E("vector", lambda eng, gg=g, s=sems["DV"]: eng.tensor_add(
                            yall[:, gg * CH:(gg + 1) * CH], yall[:, gg * CH:(gg + 1) * CH],
                            qt0[:, :]).then_inc(s, 1))
                    g_else("vector")
                    balance("vector", "DV", 1)
                    g_end("vector")
                    pend = (g, nq, S.cur("DV"))
                for t in spr.get(nq, []):
                    t()
            flush_pend()
            exp_last[kB] = S.cur("AC")
            pool_last[kB] = S.cur("PO")

        def emit_Atail(k):
            parA = k % 2
            pe_acquire("pC")
            pe_acquire("pD")
            pv = 0
            for g in range(G):
                wait("tensor", "AC", silu_g_ac[(k, g)])
                op("tensor", lambda eng, gg=g, PA=parA: eng.matmul(
                    pC[0:64, :], wxpS[:, gg * 96: gg * 96 + 64],
                    xcbP[:, PA * G * CH + gg * CH: PA * G * CH + (gg + 1) * CH],
                    start=(gg == 0), stop=(gg == G - 1)), "PE")
                pv = op("tensor", lambda eng, gg=g, PA=parA: eng.matmul(
                    pD[0:32, :], wxpS[:, gg * 96 + 64: gg * 96 + 96],
                    xcbP[:, PA * G * CH + gg * CH: PA * G * CH + (gg + 1) * CH],
                    start=(gg == 0), stop=(gg == G - 1)), "PE")
            wait("vector", "PE", pv)
            d1 = op("vector", lambda eng, PA=parA: eng.tensor_copy(
                xdrO[:, PA * CH:(PA + 1) * CH], pC[0:64, :]), "DV")
            d2 = op("vector", lambda eng, PA=parA: eng.tensor_copy(
                xbcO[:, PA * CH:(PA + 1) * CH], pD[0:32, :]), "DV")
            bank_freed("pC", "DV", d1)
            bank_freed("pD", "DV", d2)
            wait("gpsimd", "DV", d2)
            cdma_h("gpsimd", sh_xdbl.ap()[0, k][0:64, :], xdrO[:, parA * CH:(parA + 1) * CH],
                   sh_xdbl.ap()[1, k][0:64, :], xdrO[:, parA * CH:(parA + 1) * CH], "GD")
            gdp = cdma_h("gpsimd", sh_xdbl.ap()[0, k][64:96, :], xbcO[:, parA * CH:(parA + 1) * CH],
                         sh_xdbl.ap()[1, k][64:96, :], xbcO[:, parA * CH:(parA + 1) * CH], "GD")
            wait("gpsimd", "GD", gdp)
            cdma_h("gpsimd", sh_flags.ap()[0, k], magicS[:, :],
                   sh_flags.ap()[1, k], magicS[:, :], "GD")

        def emit_B3(kB):
            parB = kB % 2
            wait("vector", "ST", evac_st[(kB, 15)])
            gsp_hist = {}
            for g in range(G):
                dvg = S.inc("DV", 1)
                g_if("vector", "v_ro", 1)
                if g >= 2:
                    E("vector", lambda eng, s=sems["GD"], v=gsp_hist[g - 2]: eng.wait_ge(s, v))
                E("vector", lambda eng, gg=g, PB=parB: eng.scalar_tensor_tensor(
                    dtg[:, :], xcbP[:, PB * G * CH + gg * CH: PB * G * CH + (gg + 1) * CH],
                    DpS[:, gg:gg + 1], yall[:, gg * CH:(gg + 1) * CH], ALU.mult, ALU.add))
                E("vector", lambda eng, gg=g, PB=parB, s=sems["DV"]: eng.tensor_mul(
                    gt[:, (gg % 2) * CH:(gg % 2 + 1) * CH], dtg[:, :],
                    gzbP[:, PB * G * CH + gg * CH: PB * G * CH + (gg + 1) * CH]).then_inc(s, 1))
                g_else("vector")
                balance("vector", "DV", 1)
                g_end("vector")
                gsp = S.inc("GD", 16)
                gsp_hist[g] = gsp
                kl = kB % 16
                g_if("gpsimd", "g_ro", 1)
                E("gpsimd", lambda eng, v=dvg, s=sems["DV"]: eng.wait_ge(s, v))
                g_if("gpsimd", "g_h", 0)
                E("gpsimd", lambda eng, gg=g, KL=kl, s=sems["GD"]: eng.dma_start(
                    out=sh_gated.ap()[gg * 128:(gg + 1) * 128, KL * CH:(KL + 1) * CH],
                    in_=gt[:, (gg % 2) * CH:(gg % 2 + 1) * CH]).then_inc(s, 16))
                g_else("gpsimd")
                E("gpsimd", lambda eng, gg=g, KL=kl, s=sems["GD"]: eng.dma_start(
                    out=sh_gated.ap()[1024 + gg * 128: 1024 + (gg + 1) * 128, KL * CH:(KL + 1) * CH],
                    in_=gt[:, (gg % 2) * CH:(gg % 2 + 1) * CH]).then_inc(s, 16))
                g_end("gpsimd")
                g_else("gpsimd")
                balance("gpsimd", "GD", 16)
                g_end("gpsimd")
            g_if("vector", "v_ro", 1)
            E("vector", lambda eng, s=sems["GD"], v=gsp_hist[G - 1]: eng.wait_ge(s, v))
            g_else("vector")
            g_end("vector")

        for step in range(NCHUNK + 1):
            kA = step if step < NCHUNK else None
            kB = step - 1 if step >= 1 else None
            if kB is not None and kB in (0, 16):
                emit_roles(1 if kB == 0 else 2)
            if kA is not None:
                xload(kA, 0)
            if kB is not None:
                emit_Bhead1(kB)
            if kA is not None:
                emit_transposes(kA)
            if kB is not None:
                emit_Bhead2(kB)
            if kA is not None and kA > 0:
                for g in range(G):
                    op("vector", lambda eng, gg=g: eng.tensor_copy(
                        xi[:, gg * (CH + 3): gg * (CH + 3) + 3],
                        xi[:, gg * (CH + 3) + CH: gg * (CH + 3) + CH + 3]), "DV")
                tail_dv[kA] = S.cur("DV")
            if kB is not None:
                for nq in range(4):
                    emit_sel(kB, nq)
            if kA is not None:
                evq = make_evacs(kA)
                cvq = make_conv(kA)
                svq = make_silus(kA)
                emit_win(kA)
            else:
                evq, cvq, svq = [], [], []
            if kB is not None:
                spr = {0: evq[0:4], 1: evq[4:8],
                       2: evq[8:12] + cvq[0:2],
                       3: evq[12:16] + cvq[2:4] + svq[0:2],
                       4: cvq[4:6] + svq[2:4],
                       5: cvq[6:8] + svq[4:6],
                       6: svq[6:8], 7: []}
                emit_B2(kB, spr)
            else:
                for t in evq + cvq + svq:
                    t()
            if kA is not None:
                emit_Atail(kA)
            if kB is not None:
                emit_B3(kB)

        gall = S.cur("GD")
        wait("gpsimd", "GD", gall)
        cdma_h("gpsimd", sh_flags.ap()[0, 40], magicS[:, :],
               sh_flags.ap()[1, 40], magicS[:, :], "GD")
        poll(sh_flags.ap()[0, 40], sh_flags.ap()[1, 40])
        po_rdy = op("gpsimd", lambda eng: eng.engine_nop(), "PO")
        wait("sync", "PO", po_rdy)
        wait("tensor", "PO", po_rdy)
        wait("vector", "PO", po_rdy)
        wait("scalar", "PO", po_rdy)
        if TESTDELAY:
            for _ in range(2000):
                op("vector", lambda eng: eng.memset(dtmp[:, :], 0.0), "DV")
            wait("sync", "DV", S.cur("DV"))
            wait("gpsimd", "DV", S.cur("DV"))
            wait("tensor", "DV", S.cur("DV"))
            wait("scalar", "DV", S.cur("DV"))
        fctx.close()
        mctx.close()

        b01 = ExitStack()
        wexpS = sb("wexpS", [128, NKC * DHID], BF16, c=b01)
        wconS = sb("wconS", [128, HB * DM], BF16, c=b01)
        b0 = ExitStack()
        gbc = sb("gbc", [128, DM], c=b0)
        bbc = sb("bbc", [128, DM], c=b0)
        ld_bc = dma("sync", gbc[:, :], gbc_d.ap(), "LD")
        ld_bc = dma("sync", bbc[:, :], bbc_d.ap(), "LD")
        wait("vector", "LD", ld_bc)
        woutS = sb("woutS", [128, 16 * DM], BF16, c=b0)
        gld = sb("gld", [128, 16 * 128], BF16, c=b0)
        mt = sb("mt", [128, DM], c=b0)
        fence = sb("fence", [1, 16], BF16, c=b0)
        stat = sb("stat", [128, 8], c=b0)
        lnt = sb("lnt", [128, DM], c=b0)
        lnu = sb("lnu", [128, DM], c=b0)
        x1 = sb("x1", [128, DM], c=b0)
        xq = sb("xq", [128, DM], c=b0)

        wait("gpsimd", "PE", S.cur("PE"))
        wait("gpsimd", "AC", S.cur("AC"))
        for dch in range(2):
            gw = dma("gpsimd", woutS[:, dch * 8 * DM:(dch + 1) * 8 * DM],
                     bass.AP(Wout, dch * 8 * 128 * 1024,
                             [[1024, 128], [128 * 1024, 8], [1, 1024]]), "GD")
        wait("tensor", "GD", gw)
        prev_pv_b0 = 0
        prev_dvx1_b0 = 0
        for tt in range(32):
            if tt > 0:
                wait("sync", "PE", prev_pv_b0)
                wait("sync", "DV", prev_dvx1_b0)
            lv = cdma_h(
                "sync", gld[:, :],
                bass.AP(sh_gated, tt * 128, [[8192, 128], [128 * 8192, 16], [1, 128]]),
                gld[:, :],
                bass.AP(sh_gated, 4096 + tt * 128, [[8192, 128], [128 * 8192, 16], [1, 128]]),
                "LD")
            lx = dma("sync", xq[:, :], x_own.ap()[tt * 128:(tt + 1) * 128, :], "LD")
            lvf = dma("sync", fence[:, :], gld[0:1, 2032:2048], "LD")
            wait("tensor", "LD", lvf)
            par = tt % 2
            Pm = pA if par == 0 else pB
            nm = "pA" if par == 0 else "pB"
            pe_acquire(nm, nm + "a", nm + "b")
            for dc in range(16):
                for hf in range(2):
                    op("tensor", lambda eng, D=dc, P=Pm, O=hf * 512: eng.matmul(
                        P[:, O:O + 512], gld[:, D * 128:(D + 1) * 128],
                        woutS[:, D * DM + O: D * DM + O + 512],
                        start=(D == 0), stop=(D == 15)), "PE")
            pv = S.cur("PE")
            if tt % 4 == 3:
                qq = tt // 4
                wsrc = (wexp_d, wexpS) if qq < 4 else (wcon_d, wconS)
                qo = (qq % 4) * 8192
                dma("scalar", wsrc[1][:, qo:qo + 8192],
                    wsrc[0].ap()[:, qo:qo + 8192], "PL")
            wait("vector", "PE", pv)
            dv_m = op("vector", lambda eng, P=Pm: eng.tensor_copy(mt[:, :], P[:, :]), "DV")
            bank_freed(nm, "DV", dv_m)
            bank_freed(nm + "a", "DV", dv_m)
            bank_freed(nm + "b", "DV", dv_m)
            if DEBUG:
                wait("gpsimd", "DV", dv_m)
                gm_ = dma("gpsimd", d_m.ap()[tt * 128:(tt + 1) * 128, :], mt[:, :], "GD")
                if tt == 0:
                    dma("gpsimd", d_gld.ap(), gld[:, :], "GD")
            opd("vector", lambda eng: eng.reduce_sum(stat[:, 0:1], mt[:, :], axis=AX.X))
            wait("scalar", "DV", dv_m)
            acq = op("scalar", lambda eng: eng.activation(lnt[:, :], mt[:, :], AF.Square), "AC")
            wait("vector", "AC", acq)
            opd("vector", lambda eng: eng.reduce_sum(stat[:, 1:2], lnt[:, :], axis=AX.X))
            opd("vector", lambda eng: eng.tensor_scalar_mul(stat[:, 2:3], stat[:, 0:1], 1.0 / DM))
            opd("vector", lambda eng: eng.tensor_scalar_mul(stat[:, 3:4], stat[:, 1:2], 1.0 / DM))
            opd("vector", lambda eng: eng.tensor_mul(stat[:, 4:5], stat[:, 2:3], stat[:, 2:3]))
            dvv = opd("vector", lambda eng: eng.tensor_sub(stat[:, 4:5], stat[:, 3:4], stat[:, 4:5]))
            wait("scalar", "DV", dvv)
            acs = op("scalar", lambda eng: eng.activation(
                stat[:, 5:6], stat[:, 4:5], AF.Sqrt, bias=epsA[:, :]), "AC")
            wait("vector", "AC", acs)
            opd("vector", lambda eng: eng.reciprocal(stat[:, 6:7], stat[:, 5:6]))
            op("vector", lambda eng: eng.tensor_scalar(
                lnt[:, :], mt[:, :], stat[:, 2:3], stat[:, 6:7], ALU.subtract, ALU.mult), "DV")
            op("vector", lambda eng: eng.tensor_mul(lnt[:, :], lnt[:, :], gbc[:, :]), "DV")
            dvlt = op("vector", lambda eng: eng.tensor_add(lnt[:, :], lnt[:, :], bbc[:, :]), "DV")
            wait("vector", "LD", lx)
            op("vector", lambda eng: eng.scalar_tensor_tensor(
                lnu[:, :], xq[:, :], 2.0, lnt[:, :], ALU.mult, ALU.add), "DV")
            dvu = S.cur("DV")
            opd("vector", lambda eng: eng.reduce_sum(stat[:, 0:1], lnu[:, :], axis=AX.X))
            wait("scalar", "DV", dvu)
            if DEBUG:
                wait("scalar", "GD", S.cur("GD"))
            acq2 = op("scalar", lambda eng: eng.activation(mt[:, :], lnu[:, :], AF.Square), "AC")
            wait("vector", "AC", acq2)
            opd("vector", lambda eng: eng.reduce_sum(stat[:, 1:2], mt[:, :], axis=AX.X))
            opd("vector", lambda eng: eng.tensor_scalar_mul(stat[:, 2:3], stat[:, 0:1], 1.0 / DM))
            opd("vector", lambda eng: eng.tensor_scalar_mul(stat[:, 3:4], stat[:, 1:2], 1.0 / DM))
            opd("vector", lambda eng: eng.tensor_mul(stat[:, 4:5], stat[:, 2:3], stat[:, 2:3]))
            dvv2 = opd("vector", lambda eng: eng.tensor_sub(stat[:, 4:5], stat[:, 3:4], stat[:, 4:5]))
            wait("scalar", "DV", dvv2)
            acs2 = op("scalar", lambda eng: eng.activation(
                stat[:, 5:6], stat[:, 4:5], AF.Sqrt, bias=epsO[:, :]), "AC")
            wait("vector", "AC", acs2)
            opd("vector", lambda eng: eng.reciprocal(stat[:, 6:7], stat[:, 5:6]))
            dvx1 = op("vector", lambda eng: eng.tensor_scalar(
                x1[:, :], lnu[:, :], stat[:, 2:3], stat[:, 6:7], ALU.subtract, ALU.mult), "DV")
            if DEBUG:
                wait("gpsimd", "DV", dvx1)
                dma("gpsimd", d_stat.ap()[tt * 128:(tt + 1) * 128, :], stat[:, :], "GD")
                gl2 = dma("gpsimd", d_lnu.ap()[tt * 128:(tt + 1) * 128, :], lnu[:, :], "GD")
                wait("vector", "GD", gl2)
            wait("gpsimd", "DV", dvx1)
            gx = dma("gpsimd", x1_d.ap()[tt * 128:(tt + 1) * 128, :], x1[:, :], "GD")
            wait("vector", "GD", gx)
            gx_last_b0 = gx
            prev_pv_b0 = pv
            prev_dvx1_b0 = dvx1
        b0.close()

        b1 = ExitStack()
        b2bc = sb("b2bc", [128, DM], c=b1)
        x1t = sb("x1t", [128, 2 * DM], c=b1)
        x1b = sb("x1b", [128, DM], BF16, c=b1)
        x1fm = sb("x1fm", [128, NKC * FT], BF16, c=b1)
        gelt = sb("gelt", [128, HB * FT], BF16, c=b1)
        mt2 = sb("mt2", [128, DM], c=b1)
        gsc = sb("gsc", [128, FT], c=b1)
        stat2 = sb("stat2", [128, 8], c=b1)
        lnu2 = sb("lnu2", [128, DM], c=b1)
        outt = mt2

        wait("sync", "GD", gx_last_b0)
        ld_b2 = dma("sync", b2bc[:, :], b2bc_d.ap(), "LD")
        wait("vector", "LD", ld_b2)
        wait("tensor", "PL", S.cur("PL"))
        if DEBUG:
            wait("gpsimd", "LD", S.cur("LD"))
            gearly = dma("gpsimd", d_wexpS2.ap(), wexpS[:, :], "GD")
            wait("sync", "GD", gearly)
        prev_dvo_b1 = 0
        for ft in range(QTOK // FT):
            if ft > 0:
                wait("sync", "DV", prev_dvo_b1)
            for sub in range(2):
                lx1 = dma("sync", x1t[:, sub * DM:(sub + 1) * DM],
                          x1_d.ap()[ft * FT + sub * 128: ft * FT + (sub + 1) * 128, :], "LD")
                wait("vector", "LD", lx1)
                dvb_ = op("vector", lambda eng, Sb=sub: eng.tensor_copy(
                    x1b[:, :], x1t[:, Sb * DM:(Sb + 1) * DM]), "DV")
                wait("tensor", "DV", dvb_)
                for kc in range(NKC):
                    par = kc % 2
                    bank = "pE" if par == 0 else "pF"
                    P_ = pE if par == 0 else pF
                    pe_acquire(bank)
                    pv = op("tensor", lambda eng, K=kc, P=P_: eng.transpose(
                        P[:, 0:128], x1b[:, K * 128:(K + 1) * 128], identb[:, :]), "PE")
                    wait("vector", "PE", pv)
                    dv_ = op("vector", lambda eng, K=kc, Sb=sub, P=P_: eng.tensor_copy(
                        x1fm[:, K * FT + Sb * 128: K * FT + Sb * 128 + 128], P[:, 0:128]), "DV")
                    bank_freed(bank, "DV", dv_)
                wait("vector", "PE", S.cur("PE"))
            if DEBUG and ft == 0:
                wait("gpsimd", "DV", S.cur("DV"))
                gdbg3 = dma("gpsimd", d_x1fm.ap(), x1fm[:, :], "GD")
                gdbg4 = dma("gpsimd", d_x1t.ap(), x1t[:, :], "GD")
                wait("gpsimd", "LD", S.cur("LD"))
                gdbg5 = dma("gpsimd", d_wexpS.ap(), wexpS[:, :], "GD")
                wait("vector", "GD", gdbg5)
                wait("sync", "GD", gdbg5)
                wait("tensor", "GD", gdbg5)
            wait("tensor", "DV", S.cur("DV"))
            for hb in range(HB):
                par = hb % 2
                bank = "pC" if par == 0 else "pD"
                Ph = pC if par == 0 else pD
                pe_acquire(bank)
                for kc in range(NKC):
                    op("tensor", lambda eng, H=hb, K=kc, P=Ph: eng.matmul(
                        P[:, 0:FT], wexpS[:, K * DHID + H * 128: K * DHID + (H + 1) * 128],
                        x1fm[:, K * FT:(K + 1) * FT],
                        start=(K == 0), stop=(K == NKC - 1)), "PE")
                pv = S.cur("PE")
                wait("scalar", "PE", pv)
                if hb > 0:
                    wait("scalar", "DV", dv_gc)
                if DEBUG and ft == 0 and hb == 0:
                    dvhp = op("vector", lambda eng, P=Ph: eng.tensor_copy(
                        mt2[:, 0:FT], P[:, 0:FT]), "DV")
                    wait("gpsimd", "DV", dvhp)
                    ghp = dma("gpsimd", d_hp.ap(), mt2[:, 0:FT], "GD")
                    wait("scalar", "GD", ghp)
                acg = op("scalar", lambda eng, H=hb, P=Ph: eng.activation(
                    gsc[:, :], P[:, 0:FT], AF.Gelu, bias=beff[:, H:H + 1]), "AC")
                bank_freed(bank, "AC", acg)
                wait("vector", "AC", acg)
                dv_gc = op("vector", lambda eng, H=hb: eng.tensor_copy(
                    gelt[:, H * FT:(H + 1) * FT], gsc[:, :]), "DV")
            if DEBUG and ft == 0:
                wait("gpsimd", "DV", S.cur("DV"))
                gdbg2 = dma("gpsimd", d_gelt.ap(), gelt[:, :], "GD")
                wait("vector", "GD", gdbg2)
                wait("scalar", "GD", gdbg2)
            wait("tensor", "AC", S.cur("AC"))
            wait("tensor", "DV", S.cur("DV"))
            for s2 in range(2):
                par = s2 % 2
                Py = pA if par == 0 else pB
                nm = "pA" if par == 0 else "pB"
                pe_acquire(nm, nm + "a", nm + "b")
                for hc in range(HB):
                    for hf in range(2):
                        op("tensor", lambda eng, H=hc, S2=s2, P=Py, O=hf * 512: eng.matmul(
                            P[:, O:O + 512], gelt[:, H * FT + S2 * 128: H * FT + S2 * 128 + 128],
                            wconS[:, H * DM + O: H * DM + O + 512],
                            start=(H == 0), stop=(H == HB - 1)), "PE")
                pvy = S.cur("PE")
                wait("vector", "PE", pvy)
                if DEBUG and ft == 0 and s2 == 0:
                    dvyp = op("vector", lambda eng, P=Py: eng.tensor_copy(mt2[:, :], P[:, :]), "DV")
                    wait("gpsimd", "DV", dvyp)
                    gyp = dma("gpsimd", d_yp.ap(), mt2[:, :], "GD")
                    wait("vector", "GD", gyp)
                    wait("scalar", "GD", gyp)
                op("vector", lambda eng, S2=s2, P=Py: eng.tensor_add(
                    lnu2[:, :], P[:, :], x1t[:, S2 * DM:(S2 + 1) * DM]), "DV")
                dvu2 = op("vector", lambda eng: eng.tensor_add(lnu2[:, :], lnu2[:, :], b2bc[:, :]), "DV")
                if DEBUG and ft == 0:
                    wait("gpsimd", "DV", dvu2)
                    gdbg = dma("gpsimd", d_lnu2.ap()[s2 * 128:(s2 + 1) * 128, :], lnu2[:, :], "GD")
                    wait("vector", "GD", gdbg)
                bank_freed(nm, "DV", dvu2)
                bank_freed(nm + "a", "DV", dvu2)
                bank_freed(nm + "b", "DV", dvu2)
                opd("vector", lambda eng: eng.reduce_sum(stat2[:, 0:1], lnu2[:, :], axis=AX.X))
                wait("scalar", "DV", dvu2)
                wait("scalar", "ST", S.cur("ST"))
                ac3 = op("scalar", lambda eng: eng.activation(mt2[:, :], lnu2[:, :], AF.Square), "AC")
                wait("vector", "AC", ac3)
                opd("vector", lambda eng: eng.reduce_sum(stat2[:, 1:2], mt2[:, :], axis=AX.X))
                opd("vector", lambda eng: eng.tensor_scalar_mul(stat2[:, 2:3], stat2[:, 0:1], 1.0 / DM))
                opd("vector", lambda eng: eng.tensor_scalar_mul(stat2[:, 3:4], stat2[:, 1:2], 1.0 / DM))
                opd("vector", lambda eng: eng.tensor_mul(stat2[:, 4:5], stat2[:, 2:3], stat2[:, 2:3]))
                dvv3 = opd("vector", lambda eng: eng.tensor_sub(stat2[:, 4:5], stat2[:, 3:4], stat2[:, 4:5]))
                wait("scalar", "DV", dvv3)
                ac4 = op("scalar", lambda eng: eng.activation(
                    stat2[:, 5:6], stat2[:, 4:5], AF.Sqrt, bias=epsO[:, :]), "AC")
                wait("vector", "AC", ac4)
                opd("vector", lambda eng: eng.reciprocal(stat2[:, 6:7], stat2[:, 5:6]))
                dvo = op("vector", lambda eng: eng.tensor_scalar(
                    outt[:, :], lnu2[:, :], stat2[:, 2:3], stat2[:, 6:7],
                    ALU.subtract, ALU.mult), "DV")
                wait("sync", "DV", dvo)
                stv = dma("sync", mout.ap()[ft * FT + s2 * 128: ft * FT + (s2 + 1) * 128, :],
                          outt[:, :], "ST")
                wait("vector", "ST", stv)
                prev_dvo_b1 = dvo
        b1.close()
        b01.close()

        if DEBUG or DEBUG2 or DEBUG3:
            wait("gpsimd", "ST", S.cur("ST"))
            dma("gpsimd", d_beff.ap(), beff[:, :], "ST")
            dma("gpsimd", d_wexp.ap(), wexp_d.ap(), "ST")
            dma("gpsimd", d_wcon.ap(), wcon_d.ap(), "ST")
            dma("gpsimd", d_x1.ap(), x1_d.ap(), "ST")
            dma("gpsimd", d_gated.ap(), sh_gated.ap(), "ST")
            dma("gpsimd", d_xdbl.ap(),
                bass.AP(sh_xdbl, 0, [[CH, 2 * NCHUNK * 96], [1, CH]]), "ST")
        st_all = S.cur("ST")
        wait("gpsimd", "ST", st_all)

        with nc.Block() as block:
            @block.sync
            def _(eng):
                for fn in S.ops["sync"]:
                    fn(eng)

            @block.scalar
            def _(eng):
                for fn in S.ops["scalar"]:
                    fn(eng)

            @block.vector
            def _(eng):
                for fn in S.ops["vector"]:
                    fn(eng)

            @block.tensor
            def _(eng):
                for fn in S.ops["tensor"]:
                    fn(eng)

            @block.gpsimd
            def _(eng):
                for fn in S.ops["gpsimd"]:
                    fn(eng)

    return nc


_NC_CACHE = None


def kernel(**inputs):
    global _NC_CACHE
    f32 = np.float32

    x = np.ascontiguousarray(inputs["L0_attn_x"], f32)
    W_in = np.asarray(inputs["L0_attn_W_in"], f32)
    conv_w = np.asarray(inputs["L0_attn_conv_w"], f32)
    conv_b = np.asarray(inputs["L0_attn_conv_b"], f32)
    W_xproj = np.asarray(inputs["L0_attn_W_xproj"], f32)
    W_dt = np.asarray(inputs["L0_attn_W_dt"], f32)
    b_dt = np.asarray(inputs["L0_attn_b_dt"], f32)
    A_log = np.asarray(inputs["L0_attn_A_log"], f32)
    D_param = np.asarray(inputs["L0_attn_D_param"], f32)
    W_out = np.asarray(inputs["L0_attn_W_out"], f32)
    gamma = np.asarray(inputs["L0_attn_gamma"], f32)
    beta = np.asarray(inputs["L0_attn_beta"], f32)
    W_exp = np.asarray(inputs["L0_ffn_W_exp"], f32)
    b_exp = np.asarray(inputs["L0_ffn_b_exp"], f32)
    up_w = np.asarray(inputs["L0_ffn_up_weights"], f32)
    up_bias = np.asarray(inputs["L0_ffn_up_bias"], f32)
    dn_w = np.asarray(inputs["L0_ffn_down_weights"], f32)
    W_con = np.asarray(inputs["L0_ffn_W_con"], f32)
    b_con = np.asarray(inputs["L0_ffn_b_con"], f32)
    down_bias = np.asarray(inputs["L0_ffn_down_bias"], f32)

    wdU_, woU_ = _stage_vectors(up_w)
    wdD_, woD_ = _stage_vectors(dn_w)
    woUp_ = _perm(woU_)
    woDp_ = _perm(woD_)

    identb = np.eye(128, dtype=ml_dtypes.bfloat16)
    identf = np.eye(128, dtype=f32)
    perms = np.concatenate(
        [np.eye(128, dtype=f32)[:, np.arange(128) ^ (2 ** s)] for s in range(7)], axis=1)
    ones1 = np.ones((1, 128), f32)
    nonce = np.full((1, 16), np.random.randint(1, 2**31 - 1), np.int32)
    selmat = np.zeros((32, 32 * 128), ml_dtypes.bfloat16)
    for n in range(32):
        selmat[n, n * 128:(n + 1) * 128] = 1.0

    if _NC_CACHE is None:
        _NC_CACHE = build_program()
    nc = _NC_CACHE

    in_maps = []
    for c in range(8):
        b = c // 4
        p = (c // 2) % 2
        h = c % 2
        dlo = h * 1024
        qlo = p * 8192 + h * 4096
        xi_cols = W_in[:, dlo:dlo + 1024]
        z_cols = W_in[:, 2048 + dlo: 2048 + dlo + 1024]

        def colpack(vec):
            return np.ascontiguousarray(vec.reshape(G, 128).T, f32)

        m = {
            "x_full": np.ascontiguousarray(x[b]),
            "x_own": np.ascontiguousarray(x[b, qlo:qlo + QTOK]),
            "Wi": np.ascontiguousarray(np.concatenate([xi_cols, z_cols], axis=1)),
            "convw": np.ascontiguousarray(
                conv_w[dlo:dlo + 1024].reshape(G, 128, 4).transpose(1, 0, 2).reshape(128, G * 4)),
            "convb": colpack(conv_b[dlo:dlo + 1024]),
            "Wxp": np.ascontiguousarray(W_xproj[dlo:dlo + 1024]),
            "Wdt": np.ascontiguousarray(W_dt[:, dlo:dlo + 1024]),
            "bdt": colpack(b_dt[dlo:dlo + 1024]),
            "Alog": np.ascontiguousarray(
                A_log[dlo:dlo + 1024].reshape(G, 128, NST).transpose(1, 0, 2).reshape(128, G * NST)),
            "Dp": colpack(D_param[dlo:dlo + 1024]),
            "Wout": np.ascontiguousarray(W_out),
            "gamma": gamma.reshape(1, -1),
            "beta": beta.reshape(1, -1),
            "Wexp": W_exp,
            "bexp": b_exp.reshape(1, -1),
            "upb": np.ascontiguousarray(up_bias.reshape(HB, 128).T, f32),
            "Wcon": W_con,
            "bcon": b_con.reshape(1, -1),
            "downb": down_bias.reshape(1, -1),
            "wdU": _pack_sv(wdU_), "woU": _pack_sv(woU_), "woUp": _pack_sv(woUp_),
            "wdD": _pack_sv(wdD_), "woD": _pack_sv(woD_), "woDp": _pack_sv(woDp_),
            "identb": identb, "identf": identf, "perms": perms, "ones1": ones1,
            "selmat": selmat,
            "nonce": nonce,
            "role": np.array([[h, 1 if p == 0 else 0, 1 if p == 1 else 0,
                               0, 0, 0, 0, 0]], np.int32),
        }
        in_maps.append(m)

    res = run_bass_kernel_spmd(nc, in_maps, core_ids=list(range(8)))
    out = np.empty((2, SEQ, DM), f32)
    for c in range(8):
        b = c // 4
        p = (c // 2) % 2
        h = c % 2
        qlo = p * 8192 + h * 4096
        out[b, qlo:qlo + QTOK] = res.results[c]["mout"]
    return out
